# version 12
# baseline (speedup 1.0000x reference)
import numpy as np
import ml_dtypes
from contextlib import ExitStack

import concourse.bass as bass
import concourse.mybir as mybir
from concourse.bass_utils import run_bass_kernel_spmd

F32 = mybir.dt.float32
BF16 = mybir.dt.bfloat16
AF = mybir.ActivationFunctionType
ALU = mybir.AluOpType
AX = mybir.AxisListType

S = 2048
E = 2048
INTER = 8192
NT = S // 128
NTC = S // 512
NE = E // 128
NH = 16
NI = INTER // 128
NHG = E // 512
RSC = float(1.0 / np.sqrt(128.0))
EPS = 1e-5
CBIAS_V = -12.0


_UID = [0]


def uname(base):
    _UID[0] += 1
    return f"{base}_{_UID[0]}"


class Sem:
    def __init__(self, nc, ctx, name):
        self.h = ctx.enter_context(nc.semaphore(name))
        self.val = 0


class Prog:
    ENGINES = ("sync", "gpsimd", "tensor", "scalar", "vector")

    def __init__(self, nc):
        self.nc = nc
        self.q = {e: [] for e in self.ENGINES}

    def do(self, eng, fn, inc=None, inc_k=1):
        if inc is not None:
            inc.val += inc_k
        self.q[eng].append(("op", fn, inc, inc_k))

    def fence(self, eng, sem):
        last = self.q[eng]
        for i in range(len(last) - 1, -1, -1):
            if last[i][0] == "op":
                _, fn, inc, inc_k = last[i]
                if inc is not None:
                    self.q[eng].append(("wait", inc, inc.val))
                    return
                sem.val += 1
                last[i] = ("op", fn, sem, 1)
                break
        self.q[eng].append(("wait", sem, sem.val))

    def wait(self, eng, sem, val=None):
        v = sem.val if val is None else val
        if v > 0:
            self.q[eng].append(("wait", sem, v))

    def emit(self):
        nc = self.nc
        with nc.Block() as blk:
            for eng in self.ENGINES:
                items = self.q[eng]
                if not items:
                    continue

                def body(engobj, items=items):
                    for it in items:
                        if it[0] == "wait":
                            _, sem, v = it
                            engobj.wait_ge(sem.h, v)
                        else:
                            _, fn, inc, inc_k = it
                            instr = fn()
                            if inc is not None:
                                instr.then_inc(inc.h, inc_k)

                getattr(blk, eng)(body)


def pf3(handle, a, b):
    return handle[:].rearrange("p (a b) -> p a b", a=a, b=b)


def emit_ln(nc, sems, consts, xnt, src_ap, ctx, src_gate=None):
    sIO, sPE, sACT, sDVE, sEV = (sems[k] for k in ("io", "pe", "act", "dve", "ev"))
    st = ctx.enter_context(nc.sbuf_tensor(uname("ln_st"), [128, 8 * NT], F32))
    st2 = ctx.enter_context(nc.sbuf_tensor(uname("ln_s2"), [128, 4], F32))
    xt = [ctx.enter_context(nc.sbuf_tensor(uname(f"ln_x{i}"), [128, E], F32)) for i in range(3)]
    sq = ctx.enter_context(nc.sbuf_tensor(uname("ln_sq"), [128, E], F32))
    xn = [ctx.enter_context(nc.sbuf_tensor(uname(f"ln_n{i}"), [128, E], BF16)) for i in range(2)]
    pt = [ctx.enter_context(nc.psum_tensor(uname(f"ln_p{i}"), [128, 512], F32)) for i in range(2)]

    p = Prog(nc)
    b_io, b_pe, b_act, b_dve, b_ev = sIO.val, sPE.val, sACT.val, sDVE.val, sEV.val
    ident = consts["idbf"]
    xnt_pf = pf3(xnt, NE, S)

    p.wait("sync", sIO, b_io)
    if src_gate is None:
        p.wait("sync", sems["ev"], sems["ev"].val)
    var_done = [0] * NT
    mr_done = [0] * NT
    norm_done = [0] * NT
    pe_done = [0] * NT
    ln_cnt = [0] * NT
    for tt in range(NT + 1):
        if tt < NT:
            if tt >= 3:
                p.wait("sync", sDVE, norm_done[tt - 3])
            if src_gate is not None:
                p.wait("sync", sems["ev"], src_gate[tt])
            p.do("sync", lambda tt=tt: nc.sync.dma_start(
                xt[tt % 3][:], src_ap[tt * 128:(tt + 1) * 128, :]), inc=sIO, inc_k=16)

            x = xt[tt % 3]
            p.wait("scalar", sIO, b_io + 16 * (tt + 1))
            p.do("scalar", lambda x=x, tt=tt: nc.scalar.activation(
                sq[:], x[:], AF.Square, accum_out=st[:, 8 * tt:8 * tt + 1]),
                inc=sACT)
            ssq_done = sACT.val

            p.wait("vector", sIO, b_io + 16 * (tt + 1))
            p.do("vector", lambda x=x: nc.vector.reduce_sum(st2[:, 0:1], x[:], axis=AX.X))
            p.fence("vector", sems["fv"])
            p.do("vector", lambda tt=tt: nc.vector.tensor_scalar_mul(
                st[:, 8 * tt + 4:8 * tt + 5], st2[:, 0:1], 1.0 / E))
            p.fence("vector", sems["fv"])
            p.do("vector", lambda tt=tt: nc.vector.scalar_tensor_tensor(
                st2[:, 2:3], st[:, 8 * tt + 4:8 * tt + 5], -1.0,
                st[:, 8 * tt + 4:8 * tt + 5], op0=ALU.mult, op1=ALU.mult))
            p.fence("vector", sems["fv"])
            p.wait("vector", sACT, ssq_done)
            p.do("vector", lambda tt=tt: nc.vector.scalar_tensor_tensor(
                st[:, 8 * tt + 1:8 * tt + 2], st[:, 8 * tt:8 * tt + 1], 1.0 / E,
                st2[:, 2:3], op0=ALU.mult, op1=ALU.add), inc=sDVE)
            var_done[tt] = sDVE.val

            p.wait("scalar", sDVE, var_done[tt])
            p.do("scalar", lambda tt=tt: nc.scalar.activation(
                st2[:, 3:4], st[:, 8 * tt + 1:8 * tt + 2], AF.Ln,
                bias=consts["epsc"][:]))
            p.fence("scalar", sems["fa"])
            p.do("scalar", lambda tt=tt: nc.scalar.activation(
                st[:, 8 * tt + 2:8 * tt + 3], st2[:, 3:4], AF.Exp, scale=-0.5))
            p.fence("scalar", sems["fa"])
            p.do("scalar", lambda tt=tt: nc.scalar.activation(
                st[:, 8 * tt + 3:8 * tt + 4], st[:, 8 * tt + 2:8 * tt + 3], AF.Copy,
                scale=st[:, 8 * tt + 4:8 * tt + 5]), inc=sACT)
            mr_done[tt] = sACT.val

        td = tt - 1
        if td >= 0:
            p.wait("vector", sACT, mr_done[td])
            if td >= 2:
                p.wait("vector", sPE, pe_done[td - 2])
            p.do("vector", lambda td=td: nc.vector.tensor_scalar(
                xn[td % 2][:], xt[td % 3][:], st[:, 8 * td + 2:8 * td + 3],
                st[:, 8 * td + 3:8 * td + 4],
                op0=ALU.mult, op1=ALU.subtract), inc=sDVE)
            norm_done[td] = sDVE.val

            p.wait("tensor", sDVE, norm_done[td])
            for fg in range(4):
                G = 4 * td + fg
                if G >= 2:
                    p.wait("tensor", sEV, b_ev + G - 1)
                for j in range(4):
                    fc = 4 * fg + j
                    p.do("tensor", lambda td=td, fc=fc, fg=fg, j=j: nc.tensor.matmul(
                        pt[fg % 2][:, j * 128:(j + 1) * 128],
                        xn[td % 2][:, fc * 128:(fc + 1) * 128],
                        ident[:], start=True, stop=True),
                        inc=(sPE if j == 3 else None))

                p.wait("scalar", sPE, b_pe + 4 * td + fg + 1)
                p.do("scalar", lambda td=td, fg=fg: nc.scalar.copy(
                    xnt_pf[:, 4 * fg:4 * fg + 4, td * 128:(td + 1) * 128],
                    pf3(pt[fg % 2], 4, 128)), inc=sEV)
            ln_cnt[td] = sEV.val
            pe_done[td] = sPE.val
    p.emit()
    return ln_cnt


def emit_gateup(nc, sems, consts, xnt, wgup_ap, h_buf_ap, ctx, precast=None,
                ln_cnt=None):
    sIO, sW, sPE, sACT, sDVE = (sems[k] for k in ("io", "w", "pe", "act", "dve"))
    sPC, sST = sems["fa"], sems["ev"]
    wgst = [ctx.enter_context(nc.sbuf_tensor(uname(f"gu_w{i}"), [128, 2 * E], BF16)) for i in range(2)]
    sil = [ctx.enter_context(nc.sbuf_tensor(uname(f"gu_s{i}"), [128, 512], F32)) for i in range(2)]
    hi = [ctx.enter_context(nc.sbuf_tensor(uname(f"gu_h{i}"), [128, S], BF16)) for i in range(2)]
    pg = [ctx.enter_context(nc.psum_tensor(uname(f"gu_pg{i}"), [128, 512], F32)) for i in range(2)]
    pu = [ctx.enter_context(nc.psum_tensor(uname(f"gu_pu{i}"), [128, 512], F32)) for i in range(2)]
    pcb = None
    if precast is not None:
        pcb = [ctx.enter_context(nc.sbuf_tensor(uname(f"gu_pc{i}"), [128, 16 * 512], BF16))
               for i in range(2)]

    pc_jobs = []
    if precast is not None:
        wdn_ap, wdbf_ap, wo_ap, wobf_ap = precast
        for hg in range(NHG):
            for q in range(4):
                pc_jobs.append(("wdn", hg, q))
        for hq in range(4):
            pc_jobs.append(("wo", hq, 0))

    p = Prog(nc)
    b_io, b_w, b_pe, b_act, b_dve = sIO.val, sW.val, sPE.val, sACT.val, sDVE.val
    xnt_pf = pf3(xnt, NE, S)

    for i in range(NI):
        if i >= 2:
            p.wait("gpsimd", sPE, b_pe + 4 * (i - 1))
        p.do("gpsimd", lambda i=i: nc.gpsimd.dma_start(
            pf3(wgst[i % 2], 2, E), wgup_ap[i].transpose([1, 0, 2])),
            inc=sW, inc_k=16)

        p.wait("tensor", sW, b_w + 16 * (i + 1))
        for tcol in range(NTC):
            G = NTC * i + tcol
            par = G % 2
            if i == 0 and ln_cnt is not None:
                p.wait("tensor", sems["ev"], ln_cnt[4 * tcol + 3])
            if G >= 2:
                p.wait("tensor", sDVE, b_dve + G - 1)
            for kc in range(NE):
                p.do("tensor", lambda i=i, tcol=tcol, kc=kc, par=par: nc.tensor.matmul(
                    pg[par][:], wgst[i % 2][:, kc * 128:kc * 128 + 128],
                    xnt_pf[:, kc, tcol * 512:(tcol + 1) * 512],
                    start=(kc == 0), stop=(kc == NE - 1)))
            for kc in range(NE):
                p.do("tensor", lambda i=i, tcol=tcol, kc=kc, par=par: nc.tensor.matmul(
                    pu[par][:], wgst[i % 2][:, E + kc * 128:E + kc * 128 + 128],
                    xnt_pf[:, kc, tcol * 512:(tcol + 1) * 512],
                    start=(kc == 0), stop=(kc == NE - 1)),
                    inc=(sPE if kc == NE - 1 else None))

            p.wait("scalar", sPE, b_pe + G + 1)
            p.do("scalar", lambda G=G, par=par: nc.scalar.activation(
                sil[G % 2][:], pg[par][:], AF.Silu), inc=sACT)

            p.wait("vector", sACT, b_act + G + 1)
            if i >= 2 and tcol == 0:
                p.wait("vector", sIO, b_io + 16 * (i - 1))
            p.do("vector", lambda i=i, tcol=tcol, G=G, par=par: nc.vector.scalar_tensor_tensor(
                hi[i % 2][:, tcol * 512:(tcol + 1) * 512], pu[par][:], 0.5,
                sil[G % 2][:], op0=ALU.mult, op1=ALU.mult), inc=sDVE)

        p.wait("sync", sDVE, b_dve + NTC * (i + 1))
        p.do("sync", lambda i=i: nc.sync.dma_start(
            h_buf_ap[i], hi[i % 2][:]), inc=sIO, inc_k=16)

        if precast is not None and i < len(pc_jobs):
            kind, a, b2 = pc_jobs[i]
            par = i % 2
            if i >= 2:
                p.wait("gpsimd", sST, sems["ev"].val)
            if kind == "wdn":
                p.do("gpsimd", lambda a=a, b2=b2, par=par: nc.gpsimd.dma_start(
                    pf3(pcb[par], 16, 512),
                    precast[0][a, b2 * 16:(b2 + 1) * 16].transpose([1, 0, 2])),
                    inc=sPC, inc_k=16)
                p.wait("scalar", sPC, sPC.val)
                p.do("scalar", lambda a=a, b2=b2, par=par: nc.scalar.dma_start(
                    precast[1][a, b2 * 16:(b2 + 1) * 16].transpose([1, 0, 2]),
                    pf3(pcb[par], 16, 512)), inc=sST, inc_k=16)
            else:
                p.do("gpsimd", lambda a=a, par=par: nc.gpsimd.dma_start(
                    pf3(pcb[par], 4, 2048),
                    precast[2][a * 512:(a + 1) * 512, :].rearrange(
                        "(c p) n -> p c n", p=128)), inc=sPC, inc_k=16)
                p.wait("scalar", sPC, sPC.val)
                p.do("scalar", lambda a=a, par=par: nc.scalar.dma_start(
                    precast[3][a * 512:(a + 1) * 512, :].rearrange(
                        "(c p) n -> p c n", p=128),
                    pf3(pcb[par], 4, 2048)), inc=sST, inc_k=16)
    p.emit()


def emit_down(nc, sems, consts, wdn_ap, h_buf_ap, x_src_ap, x_dst_ap, ctx):
    sIO, sW, sPE, sDVE, sST = (sems[k] for k in ("io", "w", "pe", "dve", "ev"))
    sW2 = sems["w2"]
    TCB = 256
    NTB = S // TCB
    wd = [ctx.enter_context(nc.sbuf_tensor(uname(f"dn_w{i}"), [128, NI * 512], BF16)) for i in range(2)]
    hb = [ctx.enter_context(nc.sbuf_tensor(uname(f"dn_h{i}"), [128, NI * TCB], BF16)) for i in range(2)]
    xin = [ctx.enter_context(nc.sbuf_tensor(uname(f"dn_xi{i}"), [128, 512], F32)) for i in range(2)]
    xout = [ctx.enter_context(nc.sbuf_tensor(uname(f"dn_xo{i}"), [128, 512], F32)) for i in range(2)]
    pd = [ctx.enter_context(nc.psum_tensor(uname(f"dn_p{i}"), [128, 512], F32)) for i in range(2)]

    p = Prog(nc)
    b_io, b_w, b_w2, b_pe, b_dve, b_st = (
        sIO.val, sW.val, sW2.val, sPE.val, sDVE.val, sST.val)
    blk_cnt = [0] * NT
    p.wait("sync", sIO, b_io)
    p.wait("sync", sST, b_st)
    p.wait("scalar", sST, b_st)
    p.wait("gpsimd", sST, b_st)
    p.wait("gpsimd", sDVE, b_dve)

    for hg in range(NHG):
        wd_pf = pf3(wd[hg % 2], NI, 512)
        for q in range(4):
            if hg == 0:
                p.do("gpsimd", lambda hg=hg, q=q: nc.gpsimd.dma_start(
                    pf3(wd[0], NI, 512)[:, q * 16:(q + 1) * 16, :],
                    wdn_ap[hg, q * 16:(q + 1) * 16].transpose([1, 0, 2])),
                    inc=sW2, inc_k=16)
            else:
                if hg >= 2 and q == 0:
                    p.wait("scalar", sPE, b_pe + 16 * (hg - 1))
                p.do("scalar", lambda hg=hg, q=q: nc.scalar.dma_start(
                    pf3(wd[hg % 2], NI, 512)[:, q * 16:(q + 1) * 16, :],
                    wdn_ap[hg, q * 16:(q + 1) * 16].transpose([1, 0, 2])),
                    inc=sW, inc_k=16)

        for tc in range(NTB):
            B = hg * NTB + tc
            nG = 2 * B
            if B >= 2:
                p.wait("sync", sPE, b_pe + nG - 2)
            p.do("sync", lambda hg=hg, tc=tc: nc.sync.dma_start(
                pf3(hb[tc % 2], NI, TCB),
                h_buf_ap[:, :, tc * TCB:(tc + 1) * TCB].transpose([1, 0, 2])),
                inc=sIO, inc_k=16)
            for tt in range(2):
                gt = 2 * tc + tt
                if B >= 1:
                    p.wait("sync", sDVE, b_dve + nG - 1 + tt)
                p.do("sync", lambda hg=hg, gt=gt: nc.sync.dma_start(
                    xin[gt % 2][:],
                    x_src_ap[gt * 128:(gt + 1) * 128, hg * 512:(hg + 1) * 512]),
                    inc=sIO, inc_k=16)

            for tt in range(2):
                G = nG + tt
                par = G % 2
                p.wait("tensor", sIO, b_io + 16 * (3 * B + 2 + tt))
                if G >= 2:
                    p.wait("tensor", sDVE, b_dve + G - 1)
                for i in range(NI):
                    if tc == 0 and tt == 0 and i % 16 == 0:
                        if hg == 0:
                            p.wait("tensor", sW2, b_w2 + 16 * (i // 16 + 1))
                        else:
                            p.wait("tensor", sW,
                                   b_w + 16 * (4 * (hg - 1) + i // 16 + 1))
                    p.do("tensor", lambda i=i, tt=tt, tc=tc, par=par, hg=hg:
                         nc.tensor.matmul(
                             pd[par][:],
                             pf3(hb[tc % 2], NI, TCB)[:, i, tt * 128:(tt + 1) * 128],
                             pf3(wd[hg % 2], NI, 512)[:, i, :],
                             start=(i == 0), stop=(i == NI - 1)),
                         inc=(sPE if i == NI - 1 else None))

                p.wait("vector", sPE, b_pe + G + 1)
                if B >= 1:
                    p.wait("vector", sST, b_st + 16 * (G - 1))
                p.do("vector", lambda G=G, par=par: nc.vector.scalar_tensor_tensor(
                    xout[par][:], pd[par][:], 1.0, xin[par][:],
                    op0=ALU.mult, op1=ALU.add), inc=sDVE)

                gt = 2 * tc + tt
                p.wait("gpsimd", sDVE, b_dve + G + 1)
                p.do("gpsimd", lambda gt=gt, hg=hg, par=par: nc.gpsimd.dma_start(
                    x_dst_ap[gt * 128:(gt + 1) * 128, hg * 512:(hg + 1) * 512],
                    xout[par][:]), inc=sST, inc_k=16)
                if hg == NHG - 1:
                    blk_cnt[gt] = sST.val
    p.emit()
    return blk_cnt


def emit_attn(nc, sems, consts, xnt, wqkv_ap, ctxbuf, wobf_ap, ln_cnt, tbl_cnt, ctx):
    sIO, sW, sPE, sACT, sDVE, sPO = (
        sems[k] for k in ("io", "w", "pe", "act", "dve", "po"))
    ident = consts["idbf"]
    c2 = consts["c2"]
    spm = consts["spm"]
    mask = consts["mask"]
    onecf = consts["onecf"]
    cbias = consts["cbias"]

    wst = [ctx.enter_context(nc.sbuf_tensor(uname(f"at_w{i}"), [128, 3 * E], BF16)) for i in range(2)]
    qh = [ctx.enter_context(nc.sbuf_tensor(uname(f"at_q{i}"), [128, S], BF16)) for i in range(2)]
    kh = [ctx.enter_context(nc.sbuf_tensor(uname(f"at_k{i}"), [128, S], BF16)) for i in range(2)]
    vfm = [ctx.enter_context(nc.sbuf_tensor(uname(f"at_v{i}"), [128, S], BF16)) for i in range(2)]
    pbuf = ctx.enter_context(nc.sbuf_tensor(uname("at_pb"), [128, 2 * 1024], BF16))
    qf = [ctx.enter_context(nc.sbuf_tensor(uname(f"at_qf{i}"), [128, 512], F32)) for i in range(2)]
    qs = ctx.enter_context(nc.sbuf_tensor(uname("at_qs"), [128, 512], F32))
    t0 = ctx.enter_context(nc.sbuf_tensor(uname("at_t0"), [128, 512], F32))
    Sa = [ctx.enter_context(nc.sbuf_tensor(uname(f"at_sa{i}"), [128, 512], F32)) for i in range(2)]
    bca = ctx.enter_context(nc.sbuf_tensor(uname("at_bc"), [128, 512], F32))

    pp = [ctx.enter_context(nc.psum_tensor(uname(f"at_pp{i}"), [128, 1024], F32)) for i in range(2)]
    pctx = [ctx.enter_context(nc.psum_tensor(uname(f"at_pc{i}"), [128, 512], F32)) for i in range(2)]
    pden = [ctx.enter_context(nc.psum_tensor(uname(f"at_pd{i}"), [128, 512], F32)) for i in range(2)]

    p = Prog(nc)
    xnt_pf = pf3(xnt, NE, S)
    ctx_pf = pf3(ctxbuf, NH, S)
    wo_pf = pf3(xnt, NH, E)
    pb_pf = pf3(pbuf, 2, 1024)

    st = dict(
        unit=0,
        pair=0,
        last_ctx_mult=0,
        last_rope=0,
        wo_w_base=None,
    )
    unit_free = {}
    slot_pe = [0, 0]
    slot_po = [0, 0]
    qkv_done = [0] * (NH + 1)
    wload_done = [0] * (NH + 1)
    rope_done = [[0] * NTC for _ in range(NH + 1)]
    vv_evac = [[0, 0] for _ in range(NH + 1)]
    ctx_mult_done = [0, 0]
    den_pe = [0, 0]
    bcast_done = [0]
    pden_free = [0, 0]
    pending = []

    def unit_begin():
        u = st["unit"]
        st["unit"] += 1
        if u >= 2:
            p.wait("tensor", sACT, unit_free[u - 2])
        return u, pp[u % 2]

    def emit_wload(hb):
        gate = qkv_done[hb - 2] if hb >= 2 else 0
        if gate:
            p.wait("gpsimd", sPE, gate)
        p.do("gpsimd", lambda hb=hb: nc.gpsimd.dma_start(
            pf3(wst[hb % 2], 3, E), wqkv_ap[hb].transpose([1, 0, 2])),
            inc=sW, inc_k=16)
        wload_done[hb] = sW.val

    def emit_qk(hb, t):
        u, ppt = unit_begin()
        hp = hb % 2
        p.wait("tensor", sW, wload_done[hb])
        if hb == 0:
            p.wait("tensor", sems["ev"], ln_cnt[4 * t + 3])
        for w in range(2):
            for kc in range(NE):
                p.do("tensor", lambda w=w, kc=kc, t=t, hp=hp, u=u: nc.tensor.matmul(
                    pp[u % 2][:, w * 512:(w + 1) * 512],
                    wst[hp][:, w * E + kc * 128:w * E + kc * 128 + 128],
                    xnt_pf[:, kc, t * 512:(t + 1) * 512],
                    start=(kc == 0), stop=(kc == NE - 1)),
                    inc=(sPE if kc == NE - 1 else None))
        mm_done = sPE.val
        p.wait("scalar", sPE, mm_done)
        p.wait("scalar", sDVE, st["last_rope"])
        for w in range(2):
            p.do("scalar", lambda w=w, u=u: nc.scalar.copy(
                qf[w][:], pp[u % 2][:, w * 512:(w + 1) * 512]), inc=sACT)
        unit_free[u] = sACT.val
        p.wait("vector", sACT, sACT.val)
        for w in range(2):
            dst = qh[hp] if w == 0 else kh[hp]
            p.do("vector", lambda w=w: nc.vector.tensor_copy(
                qs[0:64, :], qf[w][64:128, :]))
            p.do("vector", lambda w=w: nc.vector.tensor_copy(
                qs[64:128, :], qf[w][0:64, :]))
            p.fence("vector", sems["fv"])
            p.do("vector", lambda w=w, t=t: nc.vector.tensor_mul(
                t0[:], qf[w][:], c2[:, t * 512:(t + 1) * 512]))
            p.do("vector", lambda w=w, t=t: nc.vector.tensor_mul(
                qf[w][:], qs[:], spm[:, t * 512:(t + 1) * 512]))
            p.fence("vector", sems["fv"])
            p.do("vector", lambda dst=dst, t=t, w=w: nc.vector.tensor_add(
                dst[:, t * 512:(t + 1) * 512], t0[:], qf[w][:]), inc=sDVE)
        rope_done[hb][t] = sDVE.val
        st["last_rope"] = sDVE.val

    def emit_vv(hb, j):
        u, ppt = unit_begin()
        hp = hb % 2
        if hb == 0:
            p.wait("tensor", sems["ev"], ln_cnt[8 * j + 7])
        for w in range(2):
            t = 2 * j + w
            for kc in range(NE):
                p.do("tensor", lambda w=w, kc=kc, t=t, hp=hp, u=u: nc.tensor.matmul(
                    pp[u % 2][:, w * 512:(w + 1) * 512],
                    wst[hp][:, 2 * E + kc * 128:2 * E + kc * 128 + 128],
                    xnt_pf[:, kc, t * 512:(t + 1) * 512],
                    start=(kc == 0), stop=(kc == NE - 1)),
                    inc=(sPE if kc == NE - 1 else None))
        mm_done = sPE.val
        if j == 1:
            qkv_done[hb] = mm_done
        p.wait("scalar", sPE, mm_done)
        p.do("scalar", lambda hp=hp, j=j, u=u: nc.scalar.copy(
            vfm[hp][:, j * 1024:(j + 1) * 1024], pp[u % 2][:]), inc=sACT)
        unit_free[u] = sACT.val
        vv_evac[hb][j] = sACT.val

    def emit_vt(hb, g):
        u, ppt = unit_begin()
        hp = hb % 2
        p.wait("tensor", sACT, vv_evac[hb][g])
        for j in range(8):
            tt = 8 * g + j
            p.do("tensor", lambda tt=tt, j=j, hp=hp, u=u: nc.tensor.matmul(
                pp[u % 2][:, j * 128:(j + 1) * 128],
                vfm[hp][:, tt * 128:(tt + 1) * 128], ident[:],
                start=True, stop=True),
                inc=(sPE if j == 7 else None))
        mm_done = sPE.val
        p.wait("scalar", sPE, mm_done)
        p.do("scalar", lambda hp=hp, g=g, u=u: nc.scalar.copy(
            vfm[hp][:, g * 1024:(g + 1) * 1024], pp[u % 2][:]), inc=sACT)
        unit_free[u] = sACT.val

    def emit_pair(h, g, pr):
        u, ppt = unit_begin()
        hp = h % 2
        diag = (2 * pr >= 4 * g)
        if pr == 0:
            p.wait("tensor", sDVE, rope_done[h][g])
        for uu in range(2):
            kc = 2 * pr + uu
            p.do("tensor", lambda kc=kc, g=g, hp=hp, u=u, uu=uu, diag=diag:
                 nc.tensor.matmul(
                     pp[u % 2][:, uu * 512:(uu + 1) * 512],
                     kh[hp][:, kc * 128:(kc + 1) * 128],
                     qh[hp][:, g * 512:(g + 1) * 512],
                     start=True, stop=not diag, skip_group_check=True),
                 inc=(None if diag else (sPE if uu == 1 else None)))
        if diag:
            j0 = 2 * pr - 4 * g
            for uu in range(2):
                p.do("tensor", lambda j0=j0, uu=uu, u=u: nc.tensor.matmul(
                    pp[u % 2][:, uu * 512:(uu + 1) * 512],
                    ident[:], mask[:, (j0 + uu) * 512:(j0 + uu + 1) * 512],
                    start=False, stop=True, skip_group_check=True),
                    inc=(sPE if uu == 1 else None))
        mm_done = sPE.val
        s = st["pair"] % 2
        st["pair"] += 1
        p.wait("scalar", sPE, mm_done)
        p.wait("scalar", sPE, slot_pe[s])
        p.wait("scalar", sPO, slot_po[s])
        p.do("scalar", lambda s=s, u=u: nc.scalar.activation(
            pb_pf[:, s, :], pp[u % 2][:], AF.Exp, bias=cbias[:], scale=RSC),
            inc=sACT)
        unit_free[u] = sACT.val
        ready = ("act", sACT.val)
        p.wait("gpsimd", sACT, ready[1])
        if pr == 0:
            p.wait("gpsimd", sPE, den_pe[g % 2])
        for half_i in range(2):
            if pr == 0 and half_i == 0:
                p.do("gpsimd", lambda s=s, g=g: nc.gpsimd.tensor_copy(
                    Sa[g % 2][:], pb_pf[:, s, 0:512]), inc=sPO)
            else:
                p.do("gpsimd", lambda s=s, g=g, half_i=half_i: nc.gpsimd.tensor_add(
                    Sa[g % 2][:], Sa[g % 2][:],
                    pb_pf[:, s, half_i * 512:(half_i + 1) * 512]), inc=sPO)
            p.wait("gpsimd", sPO, sPO.val)
        slot_po[s] = sPO.val
        return (s, ready)

    def emit_av(h, g, pr, meta, nkc):
        s, ready = meta
        hp = h % 2
        gpar = g % 2
        if pr == 0:
            p.wait("tensor", sDVE, ctx_mult_done[gpar])
        rsem = sDVE if ready[0] == "dve" else sACT
        p.wait("tensor", rsem, ready[1])
        for uu in range(2):
            kd = 2 * pr + uu
            p.do("tensor", lambda kd=kd, s=s, uu=uu, hp=hp, gpar=gpar, nkc=nkc:
                 nc.tensor.matmul(
                     pctx[gpar][:], vfm[hp][:, kd * 128:(kd + 1) * 128],
                     pb_pf[:, s, uu * 512:(uu + 1) * 512],
                     start=(kd == 0), stop=(kd == nkc - 1),
                     skip_group_check=True),
                 inc=(sPE if kd == nkc - 1 or uu == 1 else None))
        slot_pe[s] = sPE.val
        return sPE.val

    def emit_den(h, g, pool_cnt):
        p.wait("tensor", sPO, pool_cnt)
        p.wait("tensor", sDVE, pden_free[g % 2])
        p.do("tensor", lambda g=g: nc.tensor.matmul(
            pden[g % 2][0:1, :], onecf[:, 0:1], Sa[g % 2][:],
            start=True, stop=True, skip_group_check=True), inc=sPE)
        den_pe[g % 2] = sPE.val
        return sPE.val

    def flush_chain():
        if not pending:
            return
        h, g, gpar, den_cnt, av_cnt = pending.pop(0)
        p.wait("vector", sPE, den_cnt)
        p.wait("vector", sIO, bcast_done[0])
        p.do("vector", lambda g=g: nc.vector.reciprocal(
            bca[0:1, :], pden[g % 2][0:1, :]), inc=sDVE)
        pden_free[g % 2] = sDVE.val
        p.wait("sync", sDVE, pden_free[g % 2])
        p.do("sync", lambda: nc.sync.dma_start(
            bca[1:128, :],
            bca[0:1, :].rearrange("p (o n) -> p o n", o=1).broadcast_to([1, 127, 512])),
            inc=sIO, inc_k=16)
        bcast_done[0] = sIO.val
        p.wait("vector", sIO, bcast_done[0])
        p.wait("vector", sPE, av_cnt)
        p.do("vector", lambda h=h, g=g, gpar=gpar: nc.vector.tensor_mul(
            ctx_pf[:, h, g * 512:(g + 1) * 512], pctx[gpar][:], bca[:]), inc=sDVE)
        ctx_mult_done[gpar] = sDVE.val
        st["last_ctx_mult"] = sDVE.val

    p.wait("vector", sIO, tbl_cnt)
    p.wait("tensor", sIO, tbl_cnt)
    emit_wload(0)
    emit_wload(1)
    for t in range(NTC):
        emit_qk(0, t)
    for j in range(2):
        emit_vv(0, j)
    for g in range(2):
        emit_vt(0, g)

    def run_task(task):
        kind, hb, a = task
        if kind == "qk":
            emit_qk(hb, a)
        elif kind == "vv":
            emit_vv(hb, a)
        elif kind == "vt":
            emit_vt(hb, a)
        else:
            p.wait("scalar", sPE, qkv_done[NH - 1])
            for k in range(4):
                hc = 4 * hb + k
                p.do("scalar", lambda hc=hc: nc.scalar.dma_start(
                    wo_pf[:, hc, :], wobf_ap[hc * 128:(hc + 1) * 128, :]),
                    inc=sW, inc_k=16)

    den_q = []

    def drain_den():
        while den_q:
            dh, dg, pb_cnt, dav = den_q.pop(0)
            den_cnt = emit_den(dh, dg, pb_cnt)
            pending.append((dh, dg, dg % 2, den_cnt, dav))

    for h in range(NH):
        if h + 2 <= NH - 1:
            emit_wload(h + 2)
        if h + 1 <= NH - 1:
            hb = h + 1
            tasks = ([("qk", hb, t) for t in range(NTC)]
                     + [("vv", hb, j) for j in range(2)]
                     + [("vt", hb, g) for g in range(2)])
        else:
            st["wo_w_base"] = sW.val
            tasks = [("wo", b, 0) for b in range(4)]
        npairs = 20
        posset = {(i * npairs) // len(tasks) for i in range(len(tasks))}

        pidx = 0
        for g in range(NTC):
            nkc = 4 * g + 4
            npr = nkc // 2
            metas = {}
            for pr in range(npr):
                metas[pr] = emit_pair(h, g, pr)
                drain_den()
                if pr == 1:
                    flush_chain()
                if pr >= 1:
                    emit_av(h, g, pr - 1, metas[pr - 1], nkc)
                if pidx in posset and tasks:
                    run_task(tasks.pop(0))
                pidx += 1
            av_cnt = emit_av(h, g, npr - 1, metas[npr - 1], nkc)
            den_q.append((h, g, sPO.val, av_cnt))
        while tasks:
            run_task(tasks.pop(0))

    drain_den()
    while pending:
        flush_chain()
    ctx_done = sDVE.val
    p.emit()
    return st["wo_w_base"], ctx_done


def emit_wo(nc, sems, consts, ctxbuf, wo_res, wo_w_base, ctx_done,
            x_src_ap, x_dst_ap, ctx):
    sIO, sW, sPE, sDVE, sST = (sems[k] for k in ("io", "w", "pe", "dve", "ev"))
    xin = [ctx.enter_context(nc.sbuf_tensor(uname(f"wo_xi{i}"), [128, E], F32)) for i in range(2)]
    xout = [ctx.enter_context(nc.sbuf_tensor(uname(f"wo_xo{i}"), [128, E], F32)) for i in range(2)]
    pw = [ctx.enter_context(nc.psum_tensor(uname(f"wo_p{i}"), [128, 512], F32)) for i in range(2)]

    p = Prog(nc)
    b_io, b_pe, b_dve, b_st = sIO.val, sPE.val, sDVE.val, sST.val
    ctx_pf = pf3(ctxbuf, NH, S)
    wo_pf = pf3(wo_res, NH, E)
    wo_cnt = [0] * NT

    p.wait("sync", sIO, b_io)
    p.wait("sync", sST, b_st)
    for tt in range(NT):
        if tt >= 2:
            p.wait("sync", sDVE, b_dve + 4 * (tt - 1))
        p.do("sync", lambda tt=tt: nc.sync.dma_start(
            xin[tt % 2][:], x_src_ap[tt * 128:(tt + 1) * 128, :]), inc=sIO, inc_k=16)

        if tt == 0:
            p.wait("tensor", sDVE, ctx_done)
        for dcol in range(4):
            G = 4 * tt + dcol
            par = G % 2
            if G >= 2:
                p.wait("tensor", sDVE, b_dve + G - 1)
            for hc in range(NH):
                if tt == 0 and dcol == 0:
                    p.wait("tensor", sW, wo_w_base + 16 * (hc + 1))
                p.do("tensor", lambda tt=tt, dcol=dcol, hc=hc, par=par: nc.tensor.matmul(
                    pw[par][:], ctx_pf[:, hc, tt * 128:(tt + 1) * 128],
                    wo_pf[:, hc, dcol * 512:(dcol + 1) * 512],
                    start=(hc == 0), stop=(hc == NH - 1)),
                    inc=(sPE if hc == NH - 1 else None))
            p.wait("vector", sPE, b_pe + G + 1)
            p.wait("vector", sIO, b_io + 16 * (tt + 1))
            if tt >= 2:
                p.wait("vector", sST, b_st + 16 * (tt - 1))
            p.do("vector", lambda tt=tt, dcol=dcol, par=par: nc.vector.scalar_tensor_tensor(
                xout[tt % 2][:, dcol * 512:(dcol + 1) * 512], pw[par][:], 1.0,
                xin[tt % 2][:, dcol * 512:(dcol + 1) * 512],
                op0=ALU.mult, op1=ALU.add), inc=sDVE)
        p.wait("gpsimd", sDVE, b_dve + 4 * (tt + 1))
        p.do("gpsimd", lambda tt=tt: nc.gpsimd.dma_start(
            x_dst_ap[tt * 128:(tt + 1) * 128, :], xout[tt % 2][:]), inc=sST, inc_k=16)
        wo_cnt[tt] = sST.val
    p.emit()
    return wo_cnt


def build(debug=False):
    nc = bass.Bass("TRN2", target_bir_lowering=False)

    x_in = nc.dram_tensor("x", [S, E], F32, kind="ExternalInput").ap()
    wqkv = nc.dram_tensor("wqkv", [NH, 3, 128, E], F32, kind="ExternalInput").ap()
    wo = nc.dram_tensor("wo", [E, E], F32, kind="ExternalInput").ap()
    wgup = nc.dram_tensor("wgup", [NI, 2, 128, E], F32, kind="ExternalInput").ap()
    wdn = nc.dram_tensor("wdn", [NHG, NI, 128, 512], F32, kind="ExternalInput").ap()
    c2_d = nc.dram_tensor("c2", [128, S], BF16, kind="ExternalInput").ap()
    spm_d = nc.dram_tensor("spm", [128, S], BF16, kind="ExternalInput").ap()
    mask_d = nc.dram_tensor("mask", [128, 2048], BF16, kind="ExternalInput").ap()
    idbf_d = nc.dram_tensor("idbf", [128, 128], BF16, kind="ExternalInput").ap()
    onecf_d = nc.dram_tensor("onecf", [128, 1], F32, kind="ExternalInput").ap()

    y = nc.dram_tensor("y", [S, E], F32, kind="ExternalOutput").ap()
    x1 = nc.dram_tensor("x1", [S, E], F32, kind="Internal").ap()
    x2 = nc.dram_tensor("x2", [S, E], F32, kind="Internal").ap()
    h_buf = nc.dram_tensor("h_buf", [NI, 128, S], BF16, kind="Internal").ap()
    wdbf = nc.dram_tensor("wdbf", [NHG, NI, 128, 512], BF16, kind="Internal").ap()
    wobf = nc.dram_tensor("wobf", [E, E], BF16, kind="Internal").ap()

    with ExitStack() as top:
        sems = {k: Sem(nc, top, f"s_{k}")
                for k in ("io", "w", "w2", "pe", "act", "dve", "ev", "fv", "fa", "po")}

        idbf = top.enter_context(nc.sbuf_tensor("sb_idbf", [128, 128], BF16))
        onecf = top.enter_context(nc.sbuf_tensor("sb_onecf", [128, 1], F32))
        epsc = top.enter_context(nc.sbuf_tensor("sb_epsc", [128, 1], F32))
        cbias = top.enter_context(nc.sbuf_tensor("sb_cbias", [128, 1], F32))
        consts = dict(idbf=idbf, onecf=onecf, epsc=epsc, cbias=cbias)

        p = Prog(nc)
        for dst, src in ((idbf, idbf_d), (onecf, onecf_d)):
            p.do("sync", lambda dst=dst, src=src: nc.sync.dma_start(dst[:], src),
                 inc=sems["io"], inc_k=16)
        p.do("gpsimd", lambda: nc.gpsimd.memset(epsc[:], EPS), inc=sems["dve"])
        p.do("gpsimd", lambda: nc.gpsimd.memset(cbias[:], CBIAS_V), inc=sems["dve"])
        p.wait("sync", sems["io"], sems["io"].val)
        p.wait("sync", sems["dve"], sems["dve"].val)
        p.emit()

        with ExitStack() as sA:
            xnt = sA.enter_context(nc.sbuf_tensor("xnt1", [128, NE * S], BF16))
            with ExitStack() as c1:
                ln1 = emit_ln(nc, sems, consts, xnt, x_in, c1)
            with ExitStack() as c2s:
                emit_gateup(nc, sems, consts, xnt, wgup, h_buf, c2s,
                            precast=(wdn, wdbf, wo, wobf), ln_cnt=ln1)
        with ExitStack() as c3:
            dn1_cnt = emit_down(nc, sems, consts, wdbf, h_buf, x_in, x1, c3)

        with ExitStack() as sB:
            ctxbuf = sB.enter_context(nc.sbuf_tensor("ctxbuf", [128, NH * S], BF16))
            c2 = sB.enter_context(nc.sbuf_tensor("sb_c2", [128, S], BF16))
            spm = sB.enter_context(nc.sbuf_tensor("sb_spm", [128, S], BF16))
            mask = sB.enter_context(nc.sbuf_tensor("sb_mask", [128, 2048], BF16))
            consts2 = dict(consts, c2=c2, spm=spm, mask=mask)
            pt = Prog(nc)
            pt.wait("sync", sems["ev"], sems["ev"].val)
            pt.wait("sync", sems["io"], sems["io"].val)
            for dst, srcd in ((c2, c2_d), (spm, spm_d), (mask, mask_d)):
                pt.do("sync", lambda dst=dst, srcd=srcd: nc.sync.dma_start(
                    dst[:], srcd), inc=sems["io"], inc_k=16)
            pt.emit()
            tbl_cnt = sems["io"].val
            with ExitStack() as sB2:
                xnt = sB2.enter_context(nc.sbuf_tensor("xnt2", [128, NE * S], BF16))
                with ExitStack() as c4:
                    ln2 = emit_ln(nc, sems, consts2, xnt, x1, c4)
                with ExitStack() as c5:
                    wo_w_base, ctx_done = emit_attn(
                        nc, sems, consts2, xnt, wqkv, ctxbuf, wobf, ln2,
                        tbl_cnt, c5)
                with ExitStack() as c6:
                    wo_cnt = emit_wo(nc, sems, consts2, ctxbuf, xnt, wo_w_base,
                                     ctx_done, x1, x2, c6)

        with ExitStack() as sC:
            xnt = sC.enter_context(nc.sbuf_tensor("xnt3", [128, NE * S], BF16))
            with ExitStack() as c7:
                ln3 = emit_ln(nc, sems, consts, xnt, x2, c7)
            with ExitStack() as c8:
                emit_gateup(nc, sems, consts, xnt, wgup, h_buf, c8, ln_cnt=ln3)
        with ExitStack() as c9:
            emit_down(nc, sems, consts, wdbf, h_buf, x2, y, c9)

        p = Prog(nc)
        p.wait("sync", sems["io"], sems["io"].val)
        p.wait("sync", sems["w"], sems["w"].val)
        p.wait("sync", sems["w2"], sems["w2"].val)
        p.wait("sync", sems["ev"], sems["ev"].val)
        p.emit()

    return nc


def host_inputs(L0_attn_Wqkv, L0_attn_Wo, L0_ffn_w_gate_up, L0_ffn_w_down):
    f32 = np.float32
    bf16 = ml_dtypes.bfloat16

    Wqkv = np.asarray(L0_attn_Wqkv, dtype=f32)
    perm = np.concatenate([np.arange(0, 128, 2), np.arange(1, 128, 2)])
    wqkv_r = np.empty((NH, 3, 128, E), dtype=f32)
    for h in range(NH):
        for wi, base in enumerate((0, E, 2 * E)):
            m = Wqkv[:, base + h * 128: base + (h + 1) * 128]
            if wi < 2:
                m = m[:, perm]
            wqkv_r[h, wi] = m.reshape(NE, 128, 128).transpose(1, 0, 2).reshape(128, E)

    wo_r = np.ascontiguousarray(np.asarray(L0_attn_Wo, dtype=f32))

    T = np.ascontiguousarray(np.asarray(L0_ffn_w_gate_up, dtype=f32).T)
    wgup_r = np.empty((NI, 2, 128, E), dtype=f32)
    for i in range(NI):
        g = T[:, i * 128:(i + 1) * 128]
        u = T[:, INTER + i * 128: INTER + (i + 1) * 128]
        wgup_r[i, 0] = g.reshape(NE, 128, 128).transpose(1, 0, 2).reshape(128, E)
        wgup_r[i, 1] = u.reshape(NE, 128, 128).transpose(1, 0, 2).reshape(128, E)

    WT = np.ascontiguousarray(np.asarray(L0_ffn_w_down, dtype=f32).T)
    wdn_r = np.ascontiguousarray(
        WT.reshape(NI, 128, NHG, 512).transpose(2, 0, 1, 3))

    i_idx = np.arange(64, dtype=f32)
    inv_freq = (1.0 / (10000.0 ** (2.0 * i_idx / 128.0))).astype(f32)
    pos = np.arange(S, dtype=f32)
    th = np.outer(pos, inv_freq).astype(f32)
    cos = np.cos(th).astype(f32).T
    sin = np.sin(th).astype(f32).T
    c2 = np.concatenate([cos, cos], axis=0)
    spm = np.concatenate([-sin, sin], axis=0).astype(f32)

    mask = np.zeros((128, 4, 512), dtype=f32)
    kr = np.arange(128)[:, None]
    qc = np.arange(512)[None, :]
    for j in range(4):
        mask[:, j, :] = np.where(128 * j + kr <= qc, 0.0, -452.0)
    mask = np.ascontiguousarray(mask.reshape(128, 2048)).astype(bf16)

    return {
        "wqkv": wqkv_r, "wo": wo_r, "wgup": wgup_r, "wdn": wdn_r,
        "c2": np.ascontiguousarray(c2).astype(bf16),
        "spm": np.ascontiguousarray(spm).astype(bf16),
        "mask": mask,
        "idbf": np.eye(128, dtype=bf16),
        "onecf": np.ones((128, 1), dtype=np.float32),
    }


def kernel(L0_attn_x, L0_attn_Wqkv, L0_attn_Wo, L0_ffn_w_gate_up, L0_ffn_w_down):
    xs = np.ascontiguousarray(np.asarray(L0_attn_x, dtype=np.float32))
    shared = host_inputs(L0_attn_Wqkv, L0_attn_Wo, L0_ffn_w_gate_up, L0_ffn_w_down)
    nc = build(debug=False)
    in_maps = [dict(shared, x=np.ascontiguousarray(xs[b])) for b in range(8)]
    res = run_bass_kernel_spmd(nc, in_maps, core_ids=list(range(8)))
    return np.stack([r["y"] for r in res.results], axis=0)


# version 19
# speedup vs baseline: 1.0528x; 1.0528x over previous
import numpy as np
import ml_dtypes
from contextlib import ExitStack

import concourse.bass as bass
import concourse.mybir as mybir
from concourse.bass_utils import run_bass_kernel_spmd

F32 = mybir.dt.float32
BF16 = mybir.dt.bfloat16
AF = mybir.ActivationFunctionType
ALU = mybir.AluOpType
AX = mybir.AxisListType

S = 2048
E = 2048
INTER = 8192
NT = S // 128
NTC = S // 512
NE = E // 128
NH = 16
NI = INTER // 128
NHG = E // 512
RSC = float(1.0 / np.sqrt(128.0))
EPS = 1e-5
CBIAS_V = -12.0


_UID = [0]


def uname(base):
    _UID[0] += 1
    return f"{base}_{_UID[0]}"


class Sem:
    def __init__(self, nc, ctx, name):
        self.h = ctx.enter_context(nc.semaphore(name))
        self.val = 0


class Prog:
    ENGINES = ("sync", "gpsimd", "tensor", "scalar", "vector")

    def __init__(self, nc):
        self.nc = nc
        self.q = {e: [] for e in self.ENGINES}

    def do(self, eng, fn, inc=None, inc_k=1):
        if inc is not None:
            inc.val += inc_k
        self.q[eng].append(("op", fn, inc, inc_k))

    def fence(self, eng, sem):
        last = self.q[eng]
        for i in range(len(last) - 1, -1, -1):
            if last[i][0] == "op":
                _, fn, inc, inc_k = last[i]
                if inc is not None:
                    self.q[eng].append(("wait", inc, inc.val))
                    return
                sem.val += 1
                last[i] = ("op", fn, sem, 1)
                break
        self.q[eng].append(("wait", sem, sem.val))

    def wait(self, eng, sem, val=None):
        v = sem.val if val is None else val
        if v > 0:
            self.q[eng].append(("wait", sem, v))

    def emit(self):
        nc = self.nc
        with nc.Block() as blk:
            for eng in self.ENGINES:
                items = self.q[eng]
                if not items:
                    continue

                def body(engobj, items=items):
                    for it in items:
                        if it[0] == "wait":
                            _, sem, v = it
                            engobj.wait_ge(sem.h, v)
                        else:
                            _, fn, inc, inc_k = it
                            instr = fn()
                            if inc is not None:
                                instr.then_inc(inc.h, inc_k)

                getattr(blk, eng)(body)


def pf3(handle, a, b):
    return handle[:].rearrange("p (a b) -> p a b", a=a, b=b)


def emit_ln(nc, sems, consts, xnt, src_ap, ctx, src_gate=None):
    sIO, sPE, sACT, sDVE, sEV = (sems[k] for k in ("io", "pe", "act", "dve", "ev"))
    st = ctx.enter_context(nc.sbuf_tensor(uname("ln_st"), [128, 8 * NT], F32))
    st2 = ctx.enter_context(nc.sbuf_tensor(uname("ln_s2"), [128, 4], F32))
    xt = [ctx.enter_context(nc.sbuf_tensor(uname(f"ln_x{i}"), [128, E], F32)) for i in range(3)]
    sq = ctx.enter_context(nc.sbuf_tensor(uname("ln_sq"), [128, E], F32))
    xn = [ctx.enter_context(nc.sbuf_tensor(uname(f"ln_n{i}"), [128, E], BF16)) for i in range(2)]
    pt = [ctx.enter_context(nc.psum_tensor(uname(f"ln_p{i}"), [128, 512], F32)) for i in range(2)]

    p = Prog(nc)
    b_io, b_pe, b_act, b_dve, b_ev = sIO.val, sPE.val, sACT.val, sDVE.val, sEV.val
    ident = consts["idbf"]
    xnt_pf = pf3(xnt, NE, S)

    p.wait("sync", sIO, b_io)
    if src_gate is None:
        p.wait("sync", sems["ev"], sems["ev"].val)
    var_done = [0] * NT
    mr_done = [0] * NT
    norm_done = [0] * NT
    pe_done = [0] * NT
    ln_cnt = [0] * NT
    for tt in range(NT + 1):
        if tt < NT:
            if tt >= 3:
                p.wait("sync", sDVE, norm_done[tt - 3])
            if src_gate is not None:
                p.wait("sync", sems["ev"], src_gate[tt])
            p.do("sync", lambda tt=tt: nc.sync.dma_start(
                xt[tt % 3][:], src_ap[tt * 128:(tt + 1) * 128, :]), inc=sIO, inc_k=16)

            x = xt[tt % 3]
            p.wait("scalar", sIO, b_io + 16 * (tt + 1))
            p.do("scalar", lambda x=x, tt=tt: nc.scalar.activation(
                sq[:], x[:], AF.Square, accum_out=st[:, 8 * tt:8 * tt + 1]),
                inc=sACT)
            ssq_done = sACT.val

            p.wait("vector", sIO, b_io + 16 * (tt + 1))
            p.do("vector", lambda x=x: nc.vector.reduce_sum(st2[:, 0:1], x[:], axis=AX.X))
            p.fence("vector", sems["fv"])
            p.do("vector", lambda tt=tt: nc.vector.tensor_scalar_mul(
                st[:, 8 * tt + 4:8 * tt + 5], st2[:, 0:1], 1.0 / E))
            p.fence("vector", sems["fv"])
            p.do("vector", lambda tt=tt: nc.vector.scalar_tensor_tensor(
                st2[:, 2:3], st[:, 8 * tt + 4:8 * tt + 5], -1.0,
                st[:, 8 * tt + 4:8 * tt + 5], op0=ALU.mult, op1=ALU.mult))
            p.fence("vector", sems["fv"])
            p.wait("vector", sACT, ssq_done)
            p.do("vector", lambda tt=tt: nc.vector.scalar_tensor_tensor(
                st[:, 8 * tt + 1:8 * tt + 2], st[:, 8 * tt:8 * tt + 1], 1.0 / E,
                st2[:, 2:3], op0=ALU.mult, op1=ALU.add), inc=sDVE)
            var_done[tt] = sDVE.val

            p.wait("scalar", sDVE, var_done[tt])
            p.do("scalar", lambda tt=tt: nc.scalar.activation(
                st2[:, 3:4], st[:, 8 * tt + 1:8 * tt + 2], AF.Ln,
                bias=consts["epsc"][:]))
            p.fence("scalar", sems["fa"])
            p.do("scalar", lambda tt=tt: nc.scalar.activation(
                st[:, 8 * tt + 2:8 * tt + 3], st2[:, 3:4], AF.Exp, scale=-0.5))
            p.fence("scalar", sems["fa"])
            p.do("scalar", lambda tt=tt: nc.scalar.activation(
                st[:, 8 * tt + 3:8 * tt + 4], st[:, 8 * tt + 2:8 * tt + 3], AF.Copy,
                scale=st[:, 8 * tt + 4:8 * tt + 5]), inc=sACT)
            mr_done[tt] = sACT.val

        td = tt - 1
        if td >= 0:
            p.wait("vector", sACT, mr_done[td])
            if td >= 2:
                p.wait("vector", sPE, pe_done[td - 2])
            p.do("vector", lambda td=td: nc.vector.tensor_scalar(
                xn[td % 2][:], xt[td % 3][:], st[:, 8 * td + 2:8 * td + 3],
                st[:, 8 * td + 3:8 * td + 4],
                op0=ALU.mult, op1=ALU.subtract), inc=sDVE)
            norm_done[td] = sDVE.val

            p.wait("tensor", sDVE, norm_done[td])
            for fg in range(4):
                G = 4 * td + fg
                if G >= 2:
                    p.wait("tensor", sEV, b_ev + G - 1)
                for j in range(4):
                    fc = 4 * fg + j
                    p.do("tensor", lambda td=td, fc=fc, fg=fg, j=j: nc.tensor.matmul(
                        pt[fg % 2][:, j * 128:(j + 1) * 128],
                        xn[td % 2][:, fc * 128:(fc + 1) * 128],
                        ident[:], start=True, stop=True),
                        inc=(sPE if j == 3 else None))

                p.wait("scalar", sPE, b_pe + 4 * td + fg + 1)
                p.do("scalar", lambda td=td, fg=fg: nc.scalar.copy(
                    xnt_pf[:, 4 * fg:4 * fg + 4, td * 128:(td + 1) * 128],
                    pf3(pt[fg % 2], 4, 128)), inc=sEV)
            ln_cnt[td] = sEV.val
            pe_done[td] = sPE.val
    p.emit()
    return ln_cnt


def emit_gateup(nc, sems, consts, xnt, wgup_ap, h_buf_ap, ctx, precast=None,
                ln_cnt=None):
    sIO, sW, sPE, sACT, sDVE = (sems[k] for k in ("io", "w", "pe", "act", "dve"))
    sPC, sST = sems["fa"], sems["ev"]
    wgst = [ctx.enter_context(nc.sbuf_tensor(uname(f"gu_w{i}"), [128, 2 * E], BF16)) for i in range(2)]
    sil = [ctx.enter_context(nc.sbuf_tensor(uname(f"gu_s{i}"), [128, 512], F32)) for i in range(2)]
    hi = [ctx.enter_context(nc.sbuf_tensor(uname(f"gu_h{i}"), [128, S], BF16)) for i in range(2)]
    pg = [ctx.enter_context(nc.psum_tensor(uname(f"gu_pg{i}"), [128, 512], F32)) for i in range(2)]
    pu = [ctx.enter_context(nc.psum_tensor(uname(f"gu_pu{i}"), [128, 512], F32)) for i in range(2)]
    pcb = None
    if precast is not None:
        pcb = [ctx.enter_context(nc.sbuf_tensor(uname(f"gu_pc{i}"), [128, 16 * 512], BF16))
               for i in range(2)]

    pc_jobs = []
    if precast is not None:
        wdn_ap, wdbf_ap, wo_ap, wobf_ap = precast
        for hg in range(NHG):
            for q in range(4):
                pc_jobs.append(("wdn", hg, q))
        for hq in range(4):
            pc_jobs.append(("wo", hq, 0))

    p = Prog(nc)
    b_io, b_w, b_pe, b_act, b_dve = sIO.val, sW.val, sPE.val, sACT.val, sDVE.val
    xnt_pf = pf3(xnt, NE, S)

    for i in range(NI):
        if i >= 2:
            p.wait("gpsimd", sPE, b_pe + 4 * (i - 1))
        p.do("gpsimd", lambda i=i: nc.gpsimd.dma_start(
            pf3(wgst[i % 2], 2, E), wgup_ap[i].transpose([1, 0, 2])),
            inc=sW, inc_k=16)

        p.wait("tensor", sW, b_w + 16 * (i + 1))
        for tcol in range(NTC):
            G = NTC * i + tcol
            par = G % 2
            if i == 0 and ln_cnt is not None:
                p.wait("tensor", sems["ev"], ln_cnt[4 * tcol + 3])
            if G >= 2:
                p.wait("tensor", sDVE, b_dve + G - 1)
            for kc in range(NE):
                p.do("tensor", lambda i=i, tcol=tcol, kc=kc, par=par: nc.tensor.matmul(
                    pg[par][:], wgst[i % 2][:, kc * 128:kc * 128 + 128],
                    xnt_pf[:, kc, tcol * 512:(tcol + 1) * 512],
                    start=(kc == 0), stop=(kc == NE - 1)))
            for kc in range(NE):
                p.do("tensor", lambda i=i, tcol=tcol, kc=kc, par=par: nc.tensor.matmul(
                    pu[par][:], wgst[i % 2][:, E + kc * 128:E + kc * 128 + 128],
                    xnt_pf[:, kc, tcol * 512:(tcol + 1) * 512],
                    start=(kc == 0), stop=(kc == NE - 1)),
                    inc=(sPE if kc == NE - 1 else None))

            p.wait("scalar", sPE, b_pe + G + 1)
            p.do("scalar", lambda G=G, par=par: nc.scalar.activation(
                sil[G % 2][:], pg[par][:], AF.Silu), inc=sACT)

            p.wait("vector", sACT, b_act + G + 1)
            if i >= 2 and tcol == 0:
                p.wait("vector", sIO, b_io + 16 * (i - 1))
            p.do("vector", lambda i=i, tcol=tcol, G=G, par=par: nc.vector.scalar_tensor_tensor(
                hi[i % 2][:, tcol * 512:(tcol + 1) * 512], pu[par][:], 0.5,
                sil[G % 2][:], op0=ALU.mult, op1=ALU.mult), inc=sDVE)

        p.wait("sync", sDVE, b_dve + NTC * (i + 1))
        p.do("sync", lambda i=i: nc.sync.dma_start(
            h_buf_ap[i], hi[i % 2][:]), inc=sIO, inc_k=16)

        if precast is not None and i < len(pc_jobs):
            kind, a, b2 = pc_jobs[i]
            par = i % 2
            if i >= 2:
                p.wait("gpsimd", sST, sems["ev"].val)
            if kind == "wdn":
                p.do("gpsimd", lambda a=a, b2=b2, par=par: nc.gpsimd.dma_start(
                    pf3(pcb[par], 16, 512),
                    precast[0][a, b2 * 16:(b2 + 1) * 16].transpose([1, 0, 2])),
                    inc=sPC, inc_k=16)
                p.wait("scalar", sPC, sPC.val)
                p.do("scalar", lambda a=a, b2=b2, par=par: nc.scalar.dma_start(
                    precast[1][a, b2 * 16:(b2 + 1) * 16].transpose([1, 0, 2]),
                    pf3(pcb[par], 16, 512)), inc=sST, inc_k=16)
            else:
                p.do("gpsimd", lambda a=a, par=par: nc.gpsimd.dma_start(
                    pf3(pcb[par], 4, 2048),
                    precast[2][a * 512:(a + 1) * 512, :].rearrange(
                        "(c p) n -> p c n", p=128)), inc=sPC, inc_k=16)
                p.wait("scalar", sPC, sPC.val)
                p.do("scalar", lambda a=a, par=par: nc.scalar.dma_start(
                    precast[3][a * 512:(a + 1) * 512, :].rearrange(
                        "(c p) n -> p c n", p=128),
                    pf3(pcb[par], 4, 2048)), inc=sST, inc_k=16)
    p.emit()


def emit_down(nc, sems, consts, wdn_ap, h_buf_ap, x_src_ap, x_dst_ap, ctx):
    sIO, sW, sPE, sDVE, sST = (sems[k] for k in ("io", "w", "pe", "dve", "ev"))
    sW2 = sems["w2"]
    TCB = 256
    NTB = S // TCB
    wd = [ctx.enter_context(nc.sbuf_tensor(uname(f"dn_w{i}"), [128, NI * 512], BF16)) for i in range(2)]
    hb = [ctx.enter_context(nc.sbuf_tensor(uname(f"dn_h{i}"), [128, NI * TCB], BF16)) for i in range(2)]
    xin = [ctx.enter_context(nc.sbuf_tensor(uname(f"dn_xi{i}"), [128, 512], F32)) for i in range(2)]
    xout = [ctx.enter_context(nc.sbuf_tensor(uname(f"dn_xo{i}"), [128, 512], F32)) for i in range(2)]
    pd = [ctx.enter_context(nc.psum_tensor(uname(f"dn_p{i}"), [128, 512], F32)) for i in range(2)]

    p = Prog(nc)
    b_io, b_w, b_w2, b_pe, b_dve, b_st = (
        sIO.val, sW.val, sW2.val, sPE.val, sDVE.val, sST.val)
    blk_cnt = [0] * NT
    p.wait("sync", sIO, b_io)
    p.wait("sync", sST, b_st)
    p.wait("scalar", sST, b_st)
    p.wait("gpsimd", sST, b_st)
    p.wait("gpsimd", sDVE, b_dve)

    for hg in range(NHG):
        wd_pf = pf3(wd[hg % 2], NI, 512)
        for q in range(4):
            if hg >= 2 and q == 0:
                p.wait("scalar", sPE, b_pe + 16 * (hg - 1))
            p.do("scalar", lambda hg=hg, q=q: nc.scalar.dma_start(
                pf3(wd[hg % 2], NI, 512)[:, q * 16:(q + 1) * 16, :],
                wdn_ap[hg, q * 16:(q + 1) * 16].transpose([1, 0, 2])),
                inc=sW, inc_k=16)

        for tc in range(NTB):
            B = hg * NTB + tc
            nG = 2 * B
            if B >= 2:
                p.wait("sync", sPE, b_pe + nG - 2)
            p.do("sync", lambda hg=hg, tc=tc: nc.sync.dma_start(
                pf3(hb[tc % 2], NI, TCB),
                h_buf_ap[:, :, tc * TCB:(tc + 1) * TCB].transpose([1, 0, 2])),
                inc=sIO, inc_k=16)
            for tt in range(2):
                gt = 2 * tc + tt
                if B >= 1:
                    p.wait("sync", sDVE, b_dve + nG - 1 + tt)
                p.do("sync", lambda hg=hg, gt=gt: nc.sync.dma_start(
                    xin[gt % 2][:],
                    x_src_ap[gt * 128:(gt + 1) * 128, hg * 512:(hg + 1) * 512]),
                    inc=sIO, inc_k=16)

            for tt in range(2):
                G = nG + tt
                par = G % 2
                p.wait("tensor", sIO, b_io + 16 * (3 * B + 2 + tt))
                if G >= 2:
                    p.wait("tensor", sDVE, b_dve + G - 1)
                for i in range(NI):
                    if tc == 0 and tt == 0 and i % 16 == 0:
                        p.wait("tensor", sW,
                               b_w + 16 * (4 * hg + i // 16 + 1))
                    p.do("tensor", lambda i=i, tt=tt, tc=tc, par=par, hg=hg:
                         nc.tensor.matmul(
                             pd[par][:],
                             pf3(hb[tc % 2], NI, TCB)[:, i, tt * 128:(tt + 1) * 128],
                             pf3(wd[hg % 2], NI, 512)[:, i, :],
                             start=(i == 0), stop=(i == NI - 1)),
                         inc=(sPE if i == NI - 1 else None))

                p.wait("vector", sPE, b_pe + G + 1)
                if B >= 1:
                    p.wait("vector", sST, b_st + 16 * (G - 1))
                p.do("vector", lambda G=G, par=par: nc.vector.scalar_tensor_tensor(
                    xout[par][:], pd[par][:], 1.0, xin[par][:],
                    op0=ALU.mult, op1=ALU.add), inc=sDVE)

                gt = 2 * tc + tt
                p.wait("gpsimd", sDVE, b_dve + G + 1)
                p.do("gpsimd", lambda gt=gt, hg=hg, par=par: nc.gpsimd.dma_start(
                    x_dst_ap[gt * 128:(gt + 1) * 128, hg * 512:(hg + 1) * 512],
                    xout[par][:]), inc=sST, inc_k=16)
                if hg == NHG - 1:
                    blk_cnt[gt] = sST.val
    p.emit()
    return blk_cnt


def emit_attn(nc, sems, consts, xnt, wqkv_ap, ctxbuf, wobf_ap, ln_cnt, tbl_cnt, ctx):
    sIO, sW, sPE, sACT, sDVE, sPO = (
        sems[k] for k in ("io", "w", "pe", "act", "dve", "po"))
    ident = consts["idbf"]
    c2 = consts["c2"]
    spm = consts["spm"]
    mask = consts["mask"]
    onecf = consts["onecf"]
    cbias = consts["cbias"]

    wst = [ctx.enter_context(nc.sbuf_tensor(uname(f"at_w{i}"), [128, 3 * E], BF16)) for i in range(2)]
    qh = [ctx.enter_context(nc.sbuf_tensor(uname(f"at_q{i}"), [128, S], BF16)) for i in range(2)]
    kh = [ctx.enter_context(nc.sbuf_tensor(uname(f"at_k{i}"), [128, S], BF16)) for i in range(2)]
    vfm = [ctx.enter_context(nc.sbuf_tensor(uname(f"at_v{i}"), [128, S], BF16)) for i in range(2)]
    pbuf = ctx.enter_context(nc.sbuf_tensor(uname("at_pb"), [128, 2 * 1024], BF16))
    qf = [ctx.enter_context(nc.sbuf_tensor(uname(f"at_qf{i}"), [128, 512], F32)) for i in range(2)]
    qs = ctx.enter_context(nc.sbuf_tensor(uname("at_qs"), [128, 512], F32))
    t0 = ctx.enter_context(nc.sbuf_tensor(uname("at_t0"), [128, 512], F32))
    Sa = [ctx.enter_context(nc.sbuf_tensor(uname(f"at_sa{i}"), [128, 512], F32)) for i in range(2)]
    bca = ctx.enter_context(nc.sbuf_tensor(uname("at_bc"), [128, 512], F32))

    pp = [ctx.enter_context(nc.psum_tensor(uname(f"at_pp{i}"), [128, 1024], F32)) for i in range(2)]
    pctx = [ctx.enter_context(nc.psum_tensor(uname(f"at_pc{i}"), [128, 512], F32)) for i in range(2)]
    pden = [ctx.enter_context(nc.psum_tensor(uname(f"at_pd{i}"), [128, 512], F32)) for i in range(2)]

    p = Prog(nc)
    xnt_pf = pf3(xnt, NE, S)
    ctx_pf = pf3(ctxbuf, NH, S)
    wo_pf = pf3(xnt, NH, E)
    pb_pf = pf3(pbuf, 2, 1024)

    st = dict(
        unit=0,
        pair=0,
        last_ctx_mult=0,
        last_rope=0,
        wo_w_base=None,
    )
    unit_free = {}
    slot_pe = [0, 0]
    slot_po = [0, 0]
    qkv_done = [0] * (NH + 1)
    wload_done = [0] * (NH + 1)
    rope_done = [[0] * NTC for _ in range(NH + 1)]
    vv_evac = [[0, 0] for _ in range(NH + 1)]
    ctx_mult_done = [0, 0]
    den_pe = [0, 0]
    bcast_done = [0]
    pden_free = [0, 0]
    pending = []

    def unit_begin():
        u = st["unit"]
        st["unit"] += 1
        if u >= 2:
            p.wait("tensor", sACT, unit_free[u - 2])
        return u, pp[u % 2]

    def emit_wload(hb):
        gate = qkv_done[hb - 2] if hb >= 2 else 0
        if gate:
            p.wait("gpsimd", sPE, gate)
        p.do("gpsimd", lambda hb=hb: nc.gpsimd.dma_start(
            pf3(wst[hb % 2], 3, E), wqkv_ap[hb].transpose([1, 0, 2])),
            inc=sW, inc_k=16)
        wload_done[hb] = sW.val

    def emit_qk(hb, t):
        u, ppt = unit_begin()
        hp = hb % 2
        p.wait("tensor", sW, wload_done[hb])
        if hb == 0:
            p.wait("tensor", sems["ev"], ln_cnt[4 * t + 3])
        for w in range(2):
            for kc in range(NE):
                p.do("tensor", lambda w=w, kc=kc, t=t, hp=hp, u=u: nc.tensor.matmul(
                    pp[u % 2][:, w * 512:(w + 1) * 512],
                    wst[hp][:, w * E + kc * 128:w * E + kc * 128 + 128],
                    xnt_pf[:, kc, t * 512:(t + 1) * 512],
                    start=(kc == 0), stop=(kc == NE - 1)),
                    inc=(sPE if kc == NE - 1 else None))
        mm_done = sPE.val
        p.wait("scalar", sPE, mm_done)
        p.wait("scalar", sDVE, st["last_rope"])
        for w in range(2):
            p.do("scalar", lambda w=w, u=u: nc.scalar.copy(
                qf[w][:], pp[u % 2][:, w * 512:(w + 1) * 512]), inc=sACT)
        unit_free[u] = sACT.val
        p.wait("vector", sACT, sACT.val)
        for w in range(2):
            dst = qh[hp] if w == 0 else kh[hp]
            p.do("vector", lambda w=w: nc.vector.tensor_copy(
                qs[0:64, :], qf[w][64:128, :]))
            p.do("vector", lambda w=w: nc.vector.tensor_copy(
                qs[64:128, :], qf[w][0:64, :]))
            p.fence("vector", sems["fv"])
            p.do("vector", lambda w=w, t=t: nc.vector.tensor_mul(
                t0[:], qf[w][:], c2[:, t * 512:(t + 1) * 512]))
            p.do("vector", lambda w=w, t=t: nc.vector.tensor_mul(
                qf[w][:], qs[:], spm[:, t * 512:(t + 1) * 512]))
            p.fence("vector", sems["fv"])
            p.do("vector", lambda dst=dst, t=t, w=w: nc.vector.tensor_add(
                dst[:, t * 512:(t + 1) * 512], t0[:], qf[w][:]), inc=sDVE)
        rope_done[hb][t] = sDVE.val
        st["last_rope"] = sDVE.val

    def emit_vv(hb, j):
        u, ppt = unit_begin()
        hp = hb % 2
        if hb == 0:
            p.wait("tensor", sems["ev"], ln_cnt[8 * j + 7])
        for w in range(2):
            t = 2 * j + w
            for kc in range(NE):
                p.do("tensor", lambda w=w, kc=kc, t=t, hp=hp, u=u: nc.tensor.matmul(
                    pp[u % 2][:, w * 512:(w + 1) * 512],
                    wst[hp][:, 2 * E + kc * 128:2 * E + kc * 128 + 128],
                    xnt_pf[:, kc, t * 512:(t + 1) * 512],
                    start=(kc == 0), stop=(kc == NE - 1)),
                    inc=(sPE if kc == NE - 1 else None))
        mm_done = sPE.val
        if j == 1:
            qkv_done[hb] = mm_done
        p.wait("scalar", sPE, mm_done)
        p.do("scalar", lambda hp=hp, j=j, u=u: nc.scalar.copy(
            vfm[hp][:, j * 1024:(j + 1) * 1024], pp[u % 2][:]), inc=sACT)
        unit_free[u] = sACT.val
        vv_evac[hb][j] = sACT.val

    def emit_vt(hb, g):
        u, ppt = unit_begin()
        hp = hb % 2
        p.wait("tensor", sACT, vv_evac[hb][g])
        for j in range(8):
            tt = 8 * g + j
            p.do("tensor", lambda tt=tt, j=j, hp=hp, u=u: nc.tensor.matmul(
                pp[u % 2][:, j * 128:(j + 1) * 128],
                vfm[hp][:, tt * 128:(tt + 1) * 128], ident[:],
                start=True, stop=True),
                inc=(sPE if j == 7 else None))
        mm_done = sPE.val
        p.wait("scalar", sPE, mm_done)
        p.do("scalar", lambda hp=hp, g=g, u=u: nc.scalar.copy(
            vfm[hp][:, g * 1024:(g + 1) * 1024], pp[u % 2][:]), inc=sACT)
        unit_free[u] = sACT.val

    def emit_pair(h, g, pr):
        u, ppt = unit_begin()
        hp = h % 2
        diag = (2 * pr >= 4 * g)
        if pr == 0:
            p.wait("tensor", sDVE, rope_done[h][g])
        for uu in range(2):
            kc = 2 * pr + uu
            p.do("tensor", lambda kc=kc, g=g, hp=hp, u=u, uu=uu, diag=diag:
                 nc.tensor.matmul(
                     pp[u % 2][:, uu * 512:(uu + 1) * 512],
                     kh[hp][:, kc * 128:(kc + 1) * 128],
                     qh[hp][:, g * 512:(g + 1) * 512],
                     start=True, stop=not diag, skip_group_check=True),
                 inc=(None if diag else (sPE if uu == 1 else None)))
        if diag:
            j0 = 2 * pr - 4 * g
            for uu in range(2):
                p.do("tensor", lambda j0=j0, uu=uu, u=u: nc.tensor.matmul(
                    pp[u % 2][:, uu * 512:(uu + 1) * 512],
                    ident[:], mask[:, (j0 + uu) * 512:(j0 + uu + 1) * 512],
                    start=False, stop=True, skip_group_check=True),
                    inc=(sPE if uu == 1 else None))
        mm_done = sPE.val
        s = st["pair"] % 2
        st["pair"] += 1
        p.wait("scalar", sPE, mm_done)
        p.wait("scalar", sPE, slot_pe[s])
        p.wait("scalar", sPO, slot_po[s])
        p.do("scalar", lambda s=s, u=u: nc.scalar.activation(
            pb_pf[:, s, :], pp[u % 2][:], AF.Exp, bias=cbias[:], scale=RSC),
            inc=sACT)
        unit_free[u] = sACT.val
        ready = ("act", sACT.val)
        p.wait("gpsimd", sACT, ready[1])
        if pr == 0:
            p.wait("gpsimd", sPE, den_pe[g % 2])
        for half_i in range(2):
            if pr == 0 and half_i == 0:
                p.do("gpsimd", lambda s=s, g=g: nc.gpsimd.tensor_copy(
                    Sa[g % 2][:], pb_pf[:, s, 0:512]), inc=sPO)
            else:
                p.do("gpsimd", lambda s=s, g=g, half_i=half_i: nc.gpsimd.tensor_add(
                    Sa[g % 2][:], Sa[g % 2][:],
                    pb_pf[:, s, half_i * 512:(half_i + 1) * 512]), inc=sPO)
            p.wait("gpsimd", sPO, sPO.val)
        slot_po[s] = sPO.val
        return (s, ready)

    def emit_av(h, g, pr, meta, nkc):
        s, ready = meta
        hp = h % 2
        gpar = g % 2
        if pr == 0:
            p.wait("tensor", sDVE, ctx_mult_done[gpar])
        rsem = sDVE if ready[0] == "dve" else sACT
        p.wait("tensor", rsem, ready[1])
        for uu in range(2):
            kd = 2 * pr + uu
            p.do("tensor", lambda kd=kd, s=s, uu=uu, hp=hp, gpar=gpar, nkc=nkc:
                 nc.tensor.matmul(
                     pctx[gpar][:], vfm[hp][:, kd * 128:(kd + 1) * 128],
                     pb_pf[:, s, uu * 512:(uu + 1) * 512],
                     start=(kd == 0), stop=(kd == nkc - 1),
                     skip_group_check=True),
                 inc=(sPE if kd == nkc - 1 or uu == 1 else None))
        slot_pe[s] = sPE.val
        return sPE.val

    def emit_den(h, g, pool_cnt):
        p.wait("tensor", sPO, pool_cnt)
        p.wait("tensor", sDVE, pden_free[g % 2])
        p.do("tensor", lambda g=g: nc.tensor.matmul(
            pden[g % 2][0:1, :], onecf[:, 0:1], Sa[g % 2][:],
            start=True, stop=True, skip_group_check=True), inc=sPE)
        den_pe[g % 2] = sPE.val
        return sPE.val

    def flush_chain():
        if not pending:
            return
        h, g, gpar, den_cnt, av_cnt = pending.pop(0)
        p.wait("vector", sPE, den_cnt)
        p.wait("vector", sIO, bcast_done[0])
        p.do("vector", lambda g=g: nc.vector.reciprocal(
            bca[0:1, :], pden[g % 2][0:1, :]), inc=sDVE)
        pden_free[g % 2] = sDVE.val
        p.wait("sync", sDVE, pden_free[g % 2])
        p.do("sync", lambda: nc.sync.dma_start(
            bca[1:128, :],
            bca[0:1, :].rearrange("p (o n) -> p o n", o=1).broadcast_to([1, 127, 512])),
            inc=sIO, inc_k=16)
        bcast_done[0] = sIO.val
        p.wait("vector", sIO, bcast_done[0])
        p.wait("vector", sPE, av_cnt)
        p.do("vector", lambda h=h, g=g, gpar=gpar: nc.vector.tensor_mul(
            ctx_pf[:, h, g * 512:(g + 1) * 512], pctx[gpar][:], bca[:]), inc=sDVE)
        ctx_mult_done[gpar] = sDVE.val
        st["last_ctx_mult"] = sDVE.val

    p.wait("vector", sIO, tbl_cnt)
    p.wait("tensor", sIO, tbl_cnt)
    emit_wload(0)
    emit_wload(1)
    for t in range(NTC):
        emit_qk(0, t)
    for j in range(2):
        emit_vv(0, j)
    for g in range(2):
        emit_vt(0, g)

    def run_task(task):
        kind, hb, a = task
        if kind == "qk":
            emit_qk(hb, a)
        elif kind == "vv":
            emit_vv(hb, a)
        elif kind == "vt":
            emit_vt(hb, a)
        else:
            p.wait("scalar", sPE, qkv_done[NH - 1])
            for k in range(4):
                hc = 4 * hb + k
                p.do("scalar", lambda hc=hc: nc.scalar.dma_start(
                    wo_pf[:, hc, :], wobf_ap[hc * 128:(hc + 1) * 128, :]),
                    inc=sW, inc_k=16)

    den_q = []

    def drain_den():
        while den_q:
            dh, dg, pb_cnt, dav = den_q.pop(0)
            den_cnt = emit_den(dh, dg, pb_cnt)
            pending.append((dh, dg, dg % 2, den_cnt, dav))

    for h in range(NH):
        if h + 2 <= NH - 1:
            emit_wload(h + 2)
        if h + 1 <= NH - 1:
            hb = h + 1
            tasks = ([("qk", hb, t) for t in range(NTC)]
                     + [("vv", hb, j) for j in range(2)]
                     + [("vt", hb, g) for g in range(2)])
        else:
            st["wo_w_base"] = sW.val
            tasks = [("wo", b, 0) for b in range(4)]
        npairs = 20
        posset = {(i * npairs) // len(tasks) for i in range(len(tasks))}

        pidx = 0
        for g in range(NTC):
            nkc = 4 * g + 4
            npr = nkc // 2
            metas = {}
            for pr in range(npr):
                metas[pr] = emit_pair(h, g, pr)
                drain_den()
                if pr == 1:
                    flush_chain()
                if pr >= 1:
                    emit_av(h, g, pr - 1, metas[pr - 1], nkc)
                if pidx in posset and tasks:
                    run_task(tasks.pop(0))
                pidx += 1
            av_cnt = emit_av(h, g, npr - 1, metas[npr - 1], nkc)
            den_q.append((h, g, sPO.val, av_cnt))
        while tasks:
            run_task(tasks.pop(0))

    drain_den()
    while pending:
        flush_chain()
    ctx_done = sDVE.val
    p.emit()
    return st["wo_w_base"], ctx_done


def emit_wo(nc, sems, consts, ctxbuf, wo_res, wo_w_base, ctx_done,
            x_src_ap, x_dst_ap, ctx):
    sIO, sW, sPE, sDVE, sST = (sems[k] for k in ("io", "w", "pe", "dve", "ev"))
    xin = [ctx.enter_context(nc.sbuf_tensor(uname(f"wo_xi{i}"), [128, E], F32)) for i in range(2)]
    xout = [ctx.enter_context(nc.sbuf_tensor(uname(f"wo_xo{i}"), [128, E], F32)) for i in range(2)]
    pw = [ctx.enter_context(nc.psum_tensor(uname(f"wo_p{i}"), [128, 512], F32)) for i in range(2)]

    p = Prog(nc)
    b_io, b_pe, b_dve, b_st = sIO.val, sPE.val, sDVE.val, sST.val
    ctx_pf = pf3(ctxbuf, NH, S)
    wo_pf = pf3(wo_res, NH, E)
    wo_cnt = [0] * NT

    p.wait("sync", sIO, b_io)
    p.wait("sync", sST, b_st)
    for tt in range(NT):
        if tt >= 2:
            p.wait("sync", sDVE, b_dve + 4 * (tt - 1))
        p.do("sync", lambda tt=tt: nc.sync.dma_start(
            xin[tt % 2][:], x_src_ap[tt * 128:(tt + 1) * 128, :]), inc=sIO, inc_k=16)

        if tt == 0:
            p.wait("tensor", sDVE, ctx_done)
        for dcol in range(4):
            G = 4 * tt + dcol
            par = G % 2
            if G >= 2:
                p.wait("tensor", sDVE, b_dve + G - 1)
            for hc in range(NH):
                if tt == 0 and dcol == 0:
                    p.wait("tensor", sW, wo_w_base + 16 * (hc + 1))
                p.do("tensor", lambda tt=tt, dcol=dcol, hc=hc, par=par: nc.tensor.matmul(
                    pw[par][:], ctx_pf[:, hc, tt * 128:(tt + 1) * 128],
                    wo_pf[:, hc, dcol * 512:(dcol + 1) * 512],
                    start=(hc == 0), stop=(hc == NH - 1)),
                    inc=(sPE if hc == NH - 1 else None))
            p.wait("vector", sPE, b_pe + G + 1)
            p.wait("vector", sIO, b_io + 16 * (tt + 1))
            if tt >= 2:
                p.wait("vector", sST, b_st + 16 * (tt - 1))
            p.do("vector", lambda tt=tt, dcol=dcol, par=par: nc.vector.scalar_tensor_tensor(
                xout[tt % 2][:, dcol * 512:(dcol + 1) * 512], pw[par][:], 1.0,
                xin[tt % 2][:, dcol * 512:(dcol + 1) * 512],
                op0=ALU.mult, op1=ALU.add), inc=sDVE)
        p.wait("gpsimd", sDVE, b_dve + 4 * (tt + 1))
        p.do("gpsimd", lambda tt=tt: nc.gpsimd.dma_start(
            x_dst_ap[tt * 128:(tt + 1) * 128, :], xout[tt % 2][:]), inc=sST, inc_k=16)
        wo_cnt[tt] = sST.val
    p.emit()
    return wo_cnt


def build(debug=False):
    nc = bass.Bass("TRN2", target_bir_lowering=False)

    x_in = nc.dram_tensor("x", [S, E], F32, kind="ExternalInput").ap()
    wqkv = nc.dram_tensor("wqkv", [NH, 3, 128, E], F32, kind="ExternalInput").ap()
    wo = nc.dram_tensor("wo", [E, E], F32, kind="ExternalInput").ap()
    wgup = nc.dram_tensor("wgup", [NI, 2, 128, E], F32, kind="ExternalInput").ap()
    wdn = nc.dram_tensor("wdn", [NHG, NI, 128, 512], F32, kind="ExternalInput").ap()
    c2_d = nc.dram_tensor("c2", [128, S], BF16, kind="ExternalInput").ap()
    spm_d = nc.dram_tensor("spm", [128, S], BF16, kind="ExternalInput").ap()
    mask_d = nc.dram_tensor("mask", [128, 2048], BF16, kind="ExternalInput").ap()
    idbf_d = nc.dram_tensor("idbf", [128, 128], BF16, kind="ExternalInput").ap()
    onecf_d = nc.dram_tensor("onecf", [128, 1], F32, kind="ExternalInput").ap()

    y = nc.dram_tensor("y", [S, E], F32, kind="ExternalOutput").ap()
    x1 = nc.dram_tensor("x1", [S, E], F32, kind="Internal").ap()
    x2 = nc.dram_tensor("x2", [S, E], F32, kind="Internal").ap()
    h_buf = nc.dram_tensor("h_buf", [NI, 128, S], BF16, kind="Internal").ap()
    wdbf = nc.dram_tensor("wdbf", [NHG, NI, 128, 512], BF16, kind="Internal").ap()
    wobf = nc.dram_tensor("wobf", [E, E], BF16, kind="Internal").ap()

    with ExitStack() as top:
        sems = {k: Sem(nc, top, f"s_{k}")
                for k in ("io", "w", "w2", "pe", "act", "dve", "ev", "fv", "fa", "po")}

        idbf = top.enter_context(nc.sbuf_tensor("sb_idbf", [128, 128], BF16))
        onecf = top.enter_context(nc.sbuf_tensor("sb_onecf", [128, 1], F32))
        epsc = top.enter_context(nc.sbuf_tensor("sb_epsc", [128, 1], F32))
        cbias = top.enter_context(nc.sbuf_tensor("sb_cbias", [128, 1], F32))
        consts = dict(idbf=idbf, onecf=onecf, epsc=epsc, cbias=cbias)

        p = Prog(nc)
        for dst, src in ((idbf, idbf_d), (onecf, onecf_d)):
            p.do("sync", lambda dst=dst, src=src: nc.sync.dma_start(dst[:], src),
                 inc=sems["io"], inc_k=16)
        p.do("gpsimd", lambda: nc.gpsimd.memset(epsc[:], EPS), inc=sems["dve"])
        p.do("gpsimd", lambda: nc.gpsimd.memset(cbias[:], CBIAS_V), inc=sems["dve"])
        p.wait("sync", sems["io"], sems["io"].val)
        p.wait("sync", sems["dve"], sems["dve"].val)
        p.emit()

        with ExitStack() as sA:
            xnt = sA.enter_context(nc.sbuf_tensor("xnt1", [128, NE * S], BF16))
            with ExitStack() as c1:
                ln1 = emit_ln(nc, sems, consts, xnt, x_in, c1)
            with ExitStack() as c2s:
                emit_gateup(nc, sems, consts, xnt, wgup, h_buf, c2s,
                            precast=(wdn, wdbf, wo, wobf), ln_cnt=ln1)
        with ExitStack() as c3:
            dn1_cnt = emit_down(nc, sems, consts, wdbf, h_buf, x_in, x1, c3)

        with ExitStack() as sB:
            ctxbuf = sB.enter_context(nc.sbuf_tensor("ctxbuf", [128, NH * S], BF16))
            c2 = sB.enter_context(nc.sbuf_tensor("sb_c2", [128, S], BF16))
            spm = sB.enter_context(nc.sbuf_tensor("sb_spm", [128, S], BF16))
            mask = sB.enter_context(nc.sbuf_tensor("sb_mask", [128, 2048], BF16))
            consts2 = dict(consts, c2=c2, spm=spm, mask=mask)
            pt = Prog(nc)
            pt.wait("sync", sems["ev"], sems["ev"].val)
            pt.wait("sync", sems["io"], sems["io"].val)
            for dst, srcd in ((c2, c2_d), (spm, spm_d), (mask, mask_d)):
                pt.do("sync", lambda dst=dst, srcd=srcd: nc.sync.dma_start(
                    dst[:], srcd), inc=sems["io"], inc_k=16)
            pt.emit()
            tbl_cnt = sems["io"].val
            with ExitStack() as sB2:
                xnt = sB2.enter_context(nc.sbuf_tensor("xnt2", [128, NE * S], BF16))
                with ExitStack() as c4:
                    ln2 = emit_ln(nc, sems, consts2, xnt, x1, c4)
                with ExitStack() as c5:
                    wo_w_base, ctx_done = emit_attn(
                        nc, sems, consts2, xnt, wqkv, ctxbuf, wobf, ln2,
                        tbl_cnt, c5)
                with ExitStack() as c6:
                    wo_cnt = emit_wo(nc, sems, consts2, ctxbuf, xnt, wo_w_base,
                                     ctx_done, x1, x2, c6)

        with ExitStack() as sC:
            xnt = sC.enter_context(nc.sbuf_tensor("xnt3", [128, NE * S], BF16))
            with ExitStack() as c7:
                ln3 = emit_ln(nc, sems, consts, xnt, x2, c7)
            with ExitStack() as c8:
                emit_gateup(nc, sems, consts, xnt, wgup, h_buf, c8, ln_cnt=ln3)
        with ExitStack() as c9:
            emit_down(nc, sems, consts, wdbf, h_buf, x2, y, c9)

        p = Prog(nc)
        p.wait("sync", sems["io"], sems["io"].val)
        p.wait("sync", sems["w"], sems["w"].val)
        p.wait("sync", sems["w2"], sems["w2"].val)
        p.wait("sync", sems["ev"], sems["ev"].val)
        p.emit()

    return nc


def host_inputs(L0_attn_Wqkv, L0_attn_Wo, L0_ffn_w_gate_up, L0_ffn_w_down):
    f32 = np.float32
    bf16 = ml_dtypes.bfloat16

    Wqkv = np.asarray(L0_attn_Wqkv, dtype=f32)
    perm = np.concatenate([np.arange(0, 128, 2), np.arange(1, 128, 2)])
    wqkv_r = np.empty((NH, 3, 128, E), dtype=f32)
    for h in range(NH):
        for wi, base in enumerate((0, E, 2 * E)):
            m = Wqkv[:, base + h * 128: base + (h + 1) * 128]
            if wi < 2:
                m = m[:, perm]
            wqkv_r[h, wi] = m.reshape(NE, 128, 128).transpose(1, 0, 2).reshape(128, E)

    wo_r = np.ascontiguousarray(np.asarray(L0_attn_Wo, dtype=f32))

    T = np.ascontiguousarray(np.asarray(L0_ffn_w_gate_up, dtype=f32).T)
    wgup_r = np.empty((NI, 2, 128, E), dtype=f32)
    for i in range(NI):
        g = T[:, i * 128:(i + 1) * 128]
        u = T[:, INTER + i * 128: INTER + (i + 1) * 128]
        wgup_r[i, 0] = g.reshape(NE, 128, 128).transpose(1, 0, 2).reshape(128, E)
        wgup_r[i, 1] = u.reshape(NE, 128, 128).transpose(1, 0, 2).reshape(128, E)

    WT = np.ascontiguousarray(np.asarray(L0_ffn_w_down, dtype=f32).T)
    wdn_r = np.ascontiguousarray(
        WT.reshape(NI, 128, NHG, 512).transpose(2, 0, 1, 3))

    i_idx = np.arange(64, dtype=f32)
    inv_freq = (1.0 / (10000.0 ** (2.0 * i_idx / 128.0))).astype(f32)
    pos = np.arange(S, dtype=f32)
    th = np.outer(pos, inv_freq).astype(f32)
    cos = np.cos(th).astype(f32).T
    sin = np.sin(th).astype(f32).T
    c2 = np.concatenate([cos, cos], axis=0)
    spm = np.concatenate([-sin, sin], axis=0).astype(f32)

    mask = np.zeros((128, 4, 512), dtype=f32)
    kr = np.arange(128)[:, None]
    qc = np.arange(512)[None, :]
    for j in range(4):
        mask[:, j, :] = np.where(128 * j + kr <= qc, 0.0, -452.0)
    mask = np.ascontiguousarray(mask.reshape(128, 2048)).astype(bf16)

    return {
        "wqkv": wqkv_r, "wo": wo_r, "wgup": wgup_r, "wdn": wdn_r,
        "c2": np.ascontiguousarray(c2).astype(bf16),
        "spm": np.ascontiguousarray(spm).astype(bf16),
        "mask": mask,
        "idbf": np.eye(128, dtype=bf16),
        "onecf": np.ones((128, 1), dtype=np.float32),
    }


def kernel(L0_attn_x, L0_attn_Wqkv, L0_attn_Wo, L0_ffn_w_gate_up, L0_ffn_w_down):
    xs = np.ascontiguousarray(np.asarray(L0_attn_x, dtype=np.float32))
    shared = host_inputs(L0_attn_Wqkv, L0_attn_Wo, L0_ffn_w_gate_up, L0_ffn_w_down)
    nc = build(debug=False)
    in_maps = [dict(shared, x=np.ascontiguousarray(xs[b])) for b in range(8)]
    res = run_bass_kernel_spmd(nc, in_maps, core_ids=list(range(8)))
    return np.stack([r["y"] for r in res.results], axis=0)


# version 20
# speedup vs baseline: 1.0965x; 1.0415x over previous
import numpy as np
import ml_dtypes
from contextlib import ExitStack

import concourse.bass as bass
import concourse.mybir as mybir
from concourse.bass_utils import run_bass_kernel_spmd

F32 = mybir.dt.float32
BF16 = mybir.dt.bfloat16
AF = mybir.ActivationFunctionType
ALU = mybir.AluOpType
AX = mybir.AxisListType

S = 2048
E = 2048
INTER = 8192
NT = S // 128
NTC = S // 512
NE = E // 128
NH = 16
NI = INTER // 128
NHG = E // 512
RSC = float(1.0 / np.sqrt(128.0))
EPS = 1e-5
CBIAS_V = -12.0


_UID = [0]


def uname(base):
    _UID[0] += 1
    return f"{base}_{_UID[0]}"


class Sem:
    def __init__(self, nc, ctx, name):
        self.h = ctx.enter_context(nc.semaphore(name))
        self.val = 0


class Prog:
    ENGINES = ("sync", "gpsimd", "tensor", "scalar", "vector")

    def __init__(self, nc):
        self.nc = nc
        self.q = {e: [] for e in self.ENGINES}

    def do(self, eng, fn, inc=None, inc_k=1):
        if inc is not None:
            inc.val += inc_k
        self.q[eng].append(("op", fn, inc, inc_k))

    def fence(self, eng, sem):
        last = self.q[eng]
        for i in range(len(last) - 1, -1, -1):
            if last[i][0] == "op":
                _, fn, inc, inc_k = last[i]
                if inc is not None:
                    self.q[eng].append(("wait", inc, inc.val))
                    return
                sem.val += 1
                last[i] = ("op", fn, sem, 1)
                break
        self.q[eng].append(("wait", sem, sem.val))

    def wait(self, eng, sem, val=None):
        v = sem.val if val is None else val
        if v > 0:
            self.q[eng].append(("wait", sem, v))

    def emit(self):
        nc = self.nc
        with nc.Block() as blk:
            for eng in self.ENGINES:
                items = self.q[eng]
                if not items:
                    continue

                def body(engobj, items=items):
                    for it in items:
                        if it[0] == "wait":
                            _, sem, v = it
                            engobj.wait_ge(sem.h, v)
                        else:
                            _, fn, inc, inc_k = it
                            instr = fn()
                            if inc is not None:
                                instr.then_inc(inc.h, inc_k)

                getattr(blk, eng)(body)


def pf3(handle, a, b):
    return handle[:].rearrange("p (a b) -> p a b", a=a, b=b)


def emit_ln(nc, sems, consts, xnt, src_ap, ctx, src_gate=None):
    sIO, sPE, sACT, sDVE, sEV = (sems[k] for k in ("io", "pe", "act", "dve", "ev"))
    st = ctx.enter_context(nc.sbuf_tensor(uname("ln_st"), [128, 8 * NT], F32))
    st2 = ctx.enter_context(nc.sbuf_tensor(uname("ln_s2"), [128, 4], F32))
    xt = [ctx.enter_context(nc.sbuf_tensor(uname(f"ln_x{i}"), [128, E], F32)) for i in range(3)]
    sq = ctx.enter_context(nc.sbuf_tensor(uname("ln_sq"), [128, E], F32))
    xn = [ctx.enter_context(nc.sbuf_tensor(uname(f"ln_n{i}"), [128, E], BF16)) for i in range(2)]
    pt = [ctx.enter_context(nc.psum_tensor(uname(f"ln_p{i}"), [128, 512], F32)) for i in range(2)]

    p = Prog(nc)
    b_io, b_pe, b_act, b_dve, b_ev = sIO.val, sPE.val, sACT.val, sDVE.val, sEV.val
    ident = consts["idbf"]
    xnt_pf = pf3(xnt, NE, S)

    p.wait("sync", sIO, b_io)
    if src_gate is None:
        p.wait("sync", sems["ev"], sems["ev"].val)
    var_done = [0] * NT
    mr_done = [0] * NT
    norm_done = [0] * NT
    pe_done = [0] * NT
    ln_cnt = [0] * NT
    for tt in range(NT + 1):
        if tt < NT:
            if tt >= 3:
                p.wait("sync", sDVE, norm_done[tt - 3])
            if src_gate is not None:
                p.wait("sync", sems["ev"], src_gate[tt])
            p.do("sync", lambda tt=tt: nc.sync.dma_start(
                xt[tt % 3][:], src_ap[tt * 128:(tt + 1) * 128, :]), inc=sIO, inc_k=16)

            x = xt[tt % 3]
            p.wait("scalar", sIO, b_io + 16 * (tt + 1))
            p.do("scalar", lambda x=x, tt=tt: nc.scalar.activation(
                sq[:], x[:], AF.Square, accum_out=st[:, 8 * tt:8 * tt + 1]),
                inc=sACT)
            ssq_done = sACT.val

            p.wait("vector", sIO, b_io + 16 * (tt + 1))
            p.do("vector", lambda x=x: nc.vector.reduce_sum(st2[:, 0:1], x[:], axis=AX.X))
            p.fence("vector", sems["fv"])
            p.do("vector", lambda tt=tt: nc.vector.tensor_scalar_mul(
                st[:, 8 * tt + 4:8 * tt + 5], st2[:, 0:1], 1.0 / E))
            p.fence("vector", sems["fv"])
            p.do("vector", lambda tt=tt: nc.vector.scalar_tensor_tensor(
                st2[:, 2:3], st[:, 8 * tt + 4:8 * tt + 5], -1.0,
                st[:, 8 * tt + 4:8 * tt + 5], op0=ALU.mult, op1=ALU.mult))
            p.fence("vector", sems["fv"])
            p.wait("vector", sACT, ssq_done)
            p.do("vector", lambda tt=tt: nc.vector.scalar_tensor_tensor(
                st[:, 8 * tt + 1:8 * tt + 2], st[:, 8 * tt:8 * tt + 1], 1.0 / E,
                st2[:, 2:3], op0=ALU.mult, op1=ALU.add), inc=sDVE)
            var_done[tt] = sDVE.val

            p.wait("scalar", sDVE, var_done[tt])
            p.do("scalar", lambda tt=tt: nc.scalar.activation(
                st2[:, 3:4], st[:, 8 * tt + 1:8 * tt + 2], AF.Ln,
                bias=consts["epsc"][:]))
            p.fence("scalar", sems["fa"])
            p.do("scalar", lambda tt=tt: nc.scalar.activation(
                st[:, 8 * tt + 2:8 * tt + 3], st2[:, 3:4], AF.Exp, scale=-0.5))
            p.fence("scalar", sems["fa"])
            p.do("scalar", lambda tt=tt: nc.scalar.activation(
                st[:, 8 * tt + 3:8 * tt + 4], st[:, 8 * tt + 2:8 * tt + 3], AF.Copy,
                scale=st[:, 8 * tt + 4:8 * tt + 5]), inc=sACT)
            mr_done[tt] = sACT.val

        td = tt - 1
        if td >= 0:
            p.wait("vector", sACT, mr_done[td])
            if td >= 2:
                p.wait("vector", sPE, pe_done[td - 2])
            p.do("vector", lambda td=td: nc.vector.tensor_scalar(
                xn[td % 2][:], xt[td % 3][:], st[:, 8 * td + 2:8 * td + 3],
                st[:, 8 * td + 3:8 * td + 4],
                op0=ALU.mult, op1=ALU.subtract), inc=sDVE)
            norm_done[td] = sDVE.val

            p.wait("tensor", sDVE, norm_done[td])
            for fg in range(4):
                G = 4 * td + fg
                if G >= 2:
                    p.wait("tensor", sEV, b_ev + G - 1)
                for j in range(4):
                    fc = 4 * fg + j
                    p.do("tensor", lambda td=td, fc=fc, fg=fg, j=j: nc.tensor.matmul(
                        pt[fg % 2][:, j * 128:(j + 1) * 128],
                        xn[td % 2][:, fc * 128:(fc + 1) * 128],
                        ident[:], start=True, stop=True),
                        inc=(sPE if j == 3 else None))

                p.wait("scalar", sPE, b_pe + 4 * td + fg + 1)
                p.do("scalar", lambda td=td, fg=fg: nc.scalar.copy(
                    xnt_pf[:, 4 * fg:4 * fg + 4, td * 128:(td + 1) * 128],
                    pf3(pt[fg % 2], 4, 128)), inc=sEV)
            ln_cnt[td] = sEV.val
            pe_done[td] = sPE.val
    p.emit()
    return ln_cnt


def emit_gateup(nc, sems, consts, xnt, wgup_ap, h_buf_ap, ctx, precast=None,
                ln_cnt=None):
    sIO, sW, sPE, sACT, sDVE = (sems[k] for k in ("io", "w", "pe", "act", "dve"))
    sPC, sST = sems["fa"], sems["ev"]
    wgst = [ctx.enter_context(nc.sbuf_tensor(uname(f"gu_w{i}"), [128, 2 * E], BF16)) for i in range(2)]
    sil = [ctx.enter_context(nc.sbuf_tensor(uname(f"gu_s{i}"), [128, 512], F32)) for i in range(2)]
    hi = [ctx.enter_context(nc.sbuf_tensor(uname(f"gu_h{i}"), [128, S], BF16)) for i in range(2)]
    pg = [ctx.enter_context(nc.psum_tensor(uname(f"gu_pg{i}"), [128, 512], F32)) for i in range(2)]
    pu = [ctx.enter_context(nc.psum_tensor(uname(f"gu_pu{i}"), [128, 512], F32)) for i in range(2)]
    pcb = None
    if precast is not None:
        pcb = [ctx.enter_context(nc.sbuf_tensor(uname(f"gu_pc{i}"), [128, 16 * 512], BF16))
               for i in range(2)]

    pc_jobs = []
    if precast is not None:
        wdn_ap, wdbf_ap, wo_ap, wobf_ap = precast
        for hg in range(NHG):
            for q in range(4):
                pc_jobs.append(("wdn", hg, q))
        for hq in range(4):
            pc_jobs.append(("wo", hq, 0))

    p = Prog(nc)
    b_io, b_w, b_pe, b_act, b_dve = sIO.val, sW.val, sPE.val, sACT.val, sDVE.val
    xnt_pf = pf3(xnt, NE, S)

    for i in range(NI):
        if i >= 2:
            p.wait("gpsimd", sPE, b_pe + 4 * (i - 1))
        p.do("gpsimd", lambda i=i: nc.gpsimd.dma_start(
            pf3(wgst[i % 2], 2, E), wgup_ap[i].transpose([1, 0, 2])),
            inc=sW, inc_k=16)

        p.wait("tensor", sW, b_w + 16 * (i + 1))
        for tcol in range(NTC):
            G = NTC * i + tcol
            par = G % 2
            if i == 0 and ln_cnt is not None:
                p.wait("tensor", sems["ev"], ln_cnt[4 * tcol + 3])
            if G >= 2:
                p.wait("tensor", sDVE, b_dve + G - 1)
            for kc in range(NE):
                p.do("tensor", lambda i=i, tcol=tcol, kc=kc, par=par: nc.tensor.matmul(
                    pg[par][:], wgst[i % 2][:, kc * 128:kc * 128 + 128],
                    xnt_pf[:, kc, tcol * 512:(tcol + 1) * 512],
                    start=(kc == 0), stop=(kc == NE - 1)))
            for kc in range(NE):
                p.do("tensor", lambda i=i, tcol=tcol, kc=kc, par=par: nc.tensor.matmul(
                    pu[par][:], wgst[i % 2][:, E + kc * 128:E + kc * 128 + 128],
                    xnt_pf[:, kc, tcol * 512:(tcol + 1) * 512],
                    start=(kc == 0), stop=(kc == NE - 1)),
                    inc=(sPE if kc == NE - 1 else None))

            p.wait("scalar", sPE, b_pe + G + 1)
            p.do("scalar", lambda G=G, par=par: nc.scalar.activation(
                sil[G % 2][:], pg[par][:], AF.Silu), inc=sACT)

            p.wait("vector", sACT, b_act + G + 1)
            if i >= 2 and tcol == 0:
                p.wait("vector", sIO, b_io + 16 * (i - 1))
            p.do("vector", lambda i=i, tcol=tcol, G=G, par=par: nc.vector.scalar_tensor_tensor(
                hi[i % 2][:, tcol * 512:(tcol + 1) * 512], pu[par][:], 0.5,
                sil[G % 2][:], op0=ALU.mult, op1=ALU.mult), inc=sDVE)

        p.wait("sync", sDVE, b_dve + NTC * (i + 1))
        p.do("sync", lambda i=i: nc.sync.dma_start(
            h_buf_ap[i], hi[i % 2][:]), inc=sIO, inc_k=16)

        if precast is not None and i < len(pc_jobs):
            kind, a, b2 = pc_jobs[i]
            par = i % 2
            if i >= 2:
                p.wait("gpsimd", sST, sems["ev"].val)
            if kind == "wdn":
                p.do("gpsimd", lambda a=a, b2=b2, par=par: nc.gpsimd.dma_start(
                    pf3(pcb[par], 16, 512),
                    precast[0][a, b2 * 16:(b2 + 1) * 16].transpose([1, 0, 2])),
                    inc=sPC, inc_k=16)
                p.wait("scalar", sPC, sPC.val)
                p.do("scalar", lambda a=a, b2=b2, par=par: nc.scalar.dma_start(
                    precast[1][a, b2 * 16:(b2 + 1) * 16].transpose([1, 0, 2]),
                    pf3(pcb[par], 16, 512)), inc=sST, inc_k=16)
            else:
                p.do("gpsimd", lambda a=a, par=par: nc.gpsimd.dma_start(
                    pf3(pcb[par], 4, 2048),
                    precast[2][a * 512:(a + 1) * 512, :].rearrange(
                        "(c p) n -> p c n", p=128)), inc=sPC, inc_k=16)
                p.wait("scalar", sPC, sPC.val)
                p.do("scalar", lambda a=a, par=par: nc.scalar.dma_start(
                    precast[3][a * 512:(a + 1) * 512, :].rearrange(
                        "(c p) n -> p c n", p=128),
                    pf3(pcb[par], 4, 2048)), inc=sST, inc_k=16)
    p.emit()


def emit_down(nc, sems, consts, wdn_ap, h_buf_ap, x_src_ap, x_dst_ap, ctx):
    sIO, sW, sPE, sDVE, sST = (sems[k] for k in ("io", "w", "pe", "dve", "ev"))
    sW2 = sems["w2"]
    TCB = 256
    NTB = S // TCB
    wd = [ctx.enter_context(nc.sbuf_tensor(uname(f"dn_w{i}"), [128, NI * 512], BF16)) for i in range(2)]
    hb = [ctx.enter_context(nc.sbuf_tensor(uname(f"dn_h{i}"), [128, NI * TCB], BF16)) for i in range(2)]
    xin = [ctx.enter_context(nc.sbuf_tensor(uname(f"dn_xi{i}"), [128, 512], F32)) for i in range(2)]
    xout = [ctx.enter_context(nc.sbuf_tensor(uname(f"dn_xo{i}"), [128, 512], F32)) for i in range(2)]
    pd = [ctx.enter_context(nc.psum_tensor(uname(f"dn_p{i}"), [128, 512], F32)) for i in range(2)]

    p = Prog(nc)
    b_io, b_w, b_w2, b_pe, b_dve, b_st = (
        sIO.val, sW.val, sW2.val, sPE.val, sDVE.val, sST.val)
    blk_cnt = [0] * NT
    p.wait("sync", sIO, b_io)
    p.wait("sync", sST, b_st)
    p.wait("scalar", sST, b_st)
    p.wait("gpsimd", sST, b_st)
    p.wait("gpsimd", sDVE, b_dve)

    for hg in range(NHG):
        wd_pf = pf3(wd[hg % 2], NI, 512)
        for q in range(4):
            if hg >= 2 and q == 0:
                p.wait("scalar", sPE, b_pe + 16 * (hg - 1))
            p.do("scalar", lambda hg=hg, q=q: nc.scalar.dma_start(
                pf3(wd[hg % 2], NI, 512)[:, q * 16:(q + 1) * 16, :],
                wdn_ap[hg, q * 16:(q + 1) * 16].transpose([1, 0, 2])),
                inc=sW, inc_k=16)

        for tc in range(NTB):
            B = hg * NTB + tc
            nG = 2 * B
            if B >= 2:
                p.wait("sync", sPE, b_pe + nG - 2)
            p.do("sync", lambda hg=hg, tc=tc: nc.sync.dma_start(
                pf3(hb[tc % 2], NI, TCB),
                h_buf_ap[:, :, tc * TCB:(tc + 1) * TCB].transpose([1, 0, 2])),
                inc=sIO, inc_k=16)
            for tt in range(2):
                gt = 2 * tc + tt
                if B >= 1:
                    p.wait("sync", sDVE, b_dve + nG - 1 + tt)
                p.do("sync", lambda hg=hg, gt=gt: nc.sync.dma_start(
                    xin[gt % 2][:],
                    x_src_ap[gt * 128:(gt + 1) * 128, hg * 512:(hg + 1) * 512]),
                    inc=sIO, inc_k=16)

            for tt in range(2):
                G = nG + tt
                par = G % 2
                p.wait("tensor", sIO, b_io + 16 * (3 * B + 2 + tt))
                if G >= 2:
                    p.wait("tensor", sDVE, b_dve + G - 1)
                for i in range(NI):
                    if tc == 0 and tt == 0 and i % 16 == 0:
                        p.wait("tensor", sW,
                               b_w + 16 * (4 * hg + i // 16 + 1))
                    p.do("tensor", lambda i=i, tt=tt, tc=tc, par=par, hg=hg:
                         nc.tensor.matmul(
                             pd[par][:],
                             pf3(hb[tc % 2], NI, TCB)[:, i, tt * 128:(tt + 1) * 128],
                             pf3(wd[hg % 2], NI, 512)[:, i, :],
                             start=(i == 0), stop=(i == NI - 1)),
                         inc=(sPE if i == NI - 1 else None))

                p.wait("vector", sPE, b_pe + G + 1)
                if B >= 1:
                    p.wait("vector", sST, b_st + 16 * (G - 1))
                p.do("vector", lambda G=G, par=par: nc.vector.scalar_tensor_tensor(
                    xout[par][:], pd[par][:], 1.0, xin[par][:],
                    op0=ALU.mult, op1=ALU.add), inc=sDVE)

                gt = 2 * tc + tt
                p.wait("gpsimd", sDVE, b_dve + G + 1)
                p.do("gpsimd", lambda gt=gt, hg=hg, par=par: nc.gpsimd.dma_start(
                    x_dst_ap[gt * 128:(gt + 1) * 128, hg * 512:(hg + 1) * 512],
                    xout[par][:]), inc=sST, inc_k=16)
                if hg == NHG - 1:
                    blk_cnt[gt] = sST.val
    p.emit()
    return blk_cnt


def emit_attn(nc, sems, consts, xnt, wqkv_ap, ctxbuf, wobf_ap, ln_cnt, tbl_cnt, ctx):
    sIO, sW, sPE, sACT, sDVE, sPO = (
        sems[k] for k in ("io", "w", "pe", "act", "dve", "po"))
    ident = consts["idbf"]
    c2 = consts["c2"]
    spm = consts["spm"]
    mask = consts["mask"]
    onecf = consts["onecf"]
    cbias = consts["cbias"]

    wst = [ctx.enter_context(nc.sbuf_tensor(uname(f"at_w{i}"), [128, 3 * E], BF16)) for i in range(2)]
    qh = [ctx.enter_context(nc.sbuf_tensor(uname(f"at_q{i}"), [128, S], BF16)) for i in range(2)]
    kh = [ctx.enter_context(nc.sbuf_tensor(uname(f"at_k{i}"), [128, S], BF16)) for i in range(2)]
    vfm = [ctx.enter_context(nc.sbuf_tensor(uname(f"at_v{i}"), [128, S], BF16)) for i in range(2)]
    pbuf = ctx.enter_context(nc.sbuf_tensor(uname("at_pb"), [128, 2 * 1024], BF16))
    qf = [ctx.enter_context(nc.sbuf_tensor(uname(f"at_qf{i}"), [128, 512], F32)) for i in range(2)]
    qs = ctx.enter_context(nc.sbuf_tensor(uname("at_qs"), [128, 512], F32))
    t0 = ctx.enter_context(nc.sbuf_tensor(uname("at_t0"), [128, 512], F32))
    Sa = [ctx.enter_context(nc.sbuf_tensor(uname(f"at_sa{i}"), [128, 512], F32)) for i in range(2)]
    bca = ctx.enter_context(nc.sbuf_tensor(uname("at_bc"), [128, 512], F32))

    pp = [ctx.enter_context(nc.psum_tensor(uname(f"at_pp{i}"), [128, 1024], F32)) for i in range(2)]
    pctx = [ctx.enter_context(nc.psum_tensor(uname(f"at_pc{i}"), [128, 512], F32)) for i in range(2)]
    pden = [ctx.enter_context(nc.psum_tensor(uname(f"at_pd{i}"), [128, 512], F32)) for i in range(2)]

    p = Prog(nc)
    xnt_pf = pf3(xnt, NE, S)
    ctx_pf = pf3(ctxbuf, NH, S)
    wo_pf = pf3(xnt, NH, E)
    pb_pf = pf3(pbuf, 2, 1024)

    st = dict(
        unit=0,
        pair=0,
        last_ctx_mult=0,
        last_rope=0,
        wo_w_base=None,
    )
    unit_free = {}
    slot_pe = [0, 0]
    slot_po = [0, 0]
    qkv_done = [0] * (NH + 1)
    wload_done = [0] * (NH + 1)
    rope_done = [[0] * NTC for _ in range(NH + 1)]
    vv_evac = [[0, 0] for _ in range(NH + 1)]
    ctx_mult_done = [0, 0]
    den_pe = [0, 0]
    bcast_done = [0]
    pden_free = [0, 0]
    pending = []

    def unit_begin():
        u = st["unit"]
        st["unit"] += 1
        if u >= 2:
            p.wait("tensor", sACT, unit_free[u - 2])
        return u, pp[u % 2]

    def emit_wload(hb):
        gate = qkv_done[hb - 2] if hb >= 2 else 0
        if gate:
            p.wait("gpsimd", sPE, gate)
        p.do("gpsimd", lambda hb=hb: nc.gpsimd.dma_start(
            pf3(wst[hb % 2], 3, E), wqkv_ap[hb].transpose([1, 0, 2])),
            inc=sW, inc_k=16)
        wload_done[hb] = sW.val

    def emit_qk(hb, t):
        u, ppt = unit_begin()
        hp = hb % 2
        p.wait("tensor", sW, wload_done[hb])
        if hb == 0:
            p.wait("tensor", sems["ev"], ln_cnt[4 * t + 3])
        for w in range(2):
            for kc in range(NE):
                p.do("tensor", lambda w=w, kc=kc, t=t, hp=hp, u=u: nc.tensor.matmul(
                    pp[u % 2][:, w * 512:(w + 1) * 512],
                    wst[hp][:, w * E + kc * 128:w * E + kc * 128 + 128],
                    xnt_pf[:, kc, t * 512:(t + 1) * 512],
                    start=(kc == 0), stop=(kc == NE - 1)),
                    inc=(sPE if kc == NE - 1 else None))
        mm_done = sPE.val
        p.wait("scalar", sPE, mm_done)
        p.wait("scalar", sDVE, st["last_rope"])
        for w in range(2):
            p.do("scalar", lambda w=w, u=u: nc.scalar.copy(
                qf[w][:], pp[u % 2][:, w * 512:(w + 1) * 512]), inc=sACT)
        unit_free[u] = sACT.val
        p.wait("vector", sACT, sACT.val)
        for w in range(2):
            dst = qh[hp] if w == 0 else kh[hp]
            p.do("vector", lambda w=w: nc.vector.tensor_copy(
                qs[0:64, :], qf[w][64:128, :]))
            p.do("vector", lambda w=w: nc.vector.tensor_copy(
                qs[64:128, :], qf[w][0:64, :]))
            p.fence("vector", sems["fv"])
            p.do("vector", lambda w=w, t=t: nc.vector.tensor_mul(
                t0[:], qf[w][:], c2[:, t * 512:(t + 1) * 512]))
            p.do("vector", lambda w=w, t=t: nc.vector.tensor_mul(
                qf[w][:], qs[:], spm[:, t * 512:(t + 1) * 512]))
            p.fence("vector", sems["fv"])
            p.do("vector", lambda dst=dst, t=t, w=w: nc.vector.tensor_add(
                dst[:, t * 512:(t + 1) * 512], t0[:], qf[w][:]), inc=sDVE)
        rope_done[hb][t] = sDVE.val
        st["last_rope"] = sDVE.val

    def emit_vv(hb, j):
        u, ppt = unit_begin()
        hp = hb % 2
        if hb == 0:
            p.wait("tensor", sems["ev"], ln_cnt[8 * j + 7])
        for w in range(2):
            t = 2 * j + w
            for kc in range(NE):
                p.do("tensor", lambda w=w, kc=kc, t=t, hp=hp, u=u: nc.tensor.matmul(
                    pp[u % 2][:, w * 512:(w + 1) * 512],
                    wst[hp][:, 2 * E + kc * 128:2 * E + kc * 128 + 128],
                    xnt_pf[:, kc, t * 512:(t + 1) * 512],
                    start=(kc == 0), stop=(kc == NE - 1)),
                    inc=(sPE if kc == NE - 1 else None))
        mm_done = sPE.val
        if j == 1:
            qkv_done[hb] = mm_done
        p.wait("scalar", sPE, mm_done)
        p.do("scalar", lambda hp=hp, j=j, u=u: nc.scalar.copy(
            vfm[hp][:, j * 1024:(j + 1) * 1024], pp[u % 2][:]), inc=sACT)
        unit_free[u] = sACT.val
        vv_evac[hb][j] = sACT.val

    def emit_vt(hb, g):
        u, ppt = unit_begin()
        hp = hb % 2
        p.wait("tensor", sACT, vv_evac[hb][g])
        for j in range(8):
            tt = 8 * g + j
            p.do("tensor", lambda tt=tt, j=j, hp=hp, u=u: nc.tensor.matmul(
                pp[u % 2][:, j * 128:(j + 1) * 128],
                vfm[hp][:, tt * 128:(tt + 1) * 128], ident[:],
                start=True, stop=True),
                inc=(sPE if j == 7 else None))
        mm_done = sPE.val
        p.wait("scalar", sPE, mm_done)
        p.do("scalar", lambda hp=hp, g=g, u=u: nc.scalar.copy(
            vfm[hp][:, g * 1024:(g + 1) * 1024], pp[u % 2][:]), inc=sACT)
        unit_free[u] = sACT.val

    def emit_pair(h, g, pr):
        u, ppt = unit_begin()
        hp = h % 2
        diag = (2 * pr >= 4 * g)
        if pr == 0:
            p.wait("tensor", sDVE, rope_done[h][g])
        for uu in range(2):
            kc = 2 * pr + uu
            p.do("tensor", lambda kc=kc, g=g, hp=hp, u=u, uu=uu, diag=diag:
                 nc.tensor.matmul(
                     pp[u % 2][:, uu * 512:(uu + 1) * 512],
                     kh[hp][:, kc * 128:(kc + 1) * 128],
                     qh[hp][:, g * 512:(g + 1) * 512],
                     start=True, stop=not diag, skip_group_check=True),
                 inc=(None if diag else (sPE if uu == 1 else None)))
        if diag:
            j0 = 2 * pr - 4 * g
            for uu in range(2):
                p.do("tensor", lambda j0=j0, uu=uu, u=u: nc.tensor.matmul(
                    pp[u % 2][:, uu * 512:(uu + 1) * 512],
                    ident[:], mask[:, (j0 + uu) * 512:(j0 + uu + 1) * 512],
                    start=False, stop=True, skip_group_check=True),
                    inc=(sPE if uu == 1 else None))
        mm_done = sPE.val
        s = st["pair"] % 2
        st["pair"] += 1
        p.wait("scalar", sPE, mm_done)
        p.wait("scalar", sPE, slot_pe[s])
        p.wait("scalar", sPO, slot_po[s])
        p.do("scalar", lambda s=s, u=u: nc.scalar.activation(
            pb_pf[:, s, :], pp[u % 2][:], AF.Exp, bias=cbias[:], scale=RSC),
            inc=sACT)
        unit_free[u] = sACT.val
        ready = ("act", sACT.val)
        p.wait("gpsimd", sACT, ready[1])
        if pr == 0:
            p.wait("gpsimd", sPE, den_pe[g % 2])
        for half_i in range(2):
            if pr == 0 and half_i == 0:
                p.do("gpsimd", lambda s=s, g=g: nc.gpsimd.tensor_copy(
                    Sa[g % 2][:], pb_pf[:, s, 0:512]), inc=sPO)
            else:
                p.do("gpsimd", lambda s=s, g=g, half_i=half_i: nc.gpsimd.tensor_add(
                    Sa[g % 2][:], Sa[g % 2][:],
                    pb_pf[:, s, half_i * 512:(half_i + 1) * 512]), inc=sPO)
            p.wait("gpsimd", sPO, sPO.val)
        slot_po[s] = sPO.val
        return (s, ready)

    def emit_av(h, g, pr, meta, nkc):
        s, ready = meta
        hp = h % 2
        gpar = g % 2
        if pr == 0:
            p.wait("tensor", sDVE, ctx_mult_done[gpar])
        rsem = sDVE if ready[0] == "dve" else sACT
        p.wait("tensor", rsem, ready[1])
        for uu in range(2):
            kd = 2 * pr + uu
            p.do("tensor", lambda kd=kd, s=s, uu=uu, hp=hp, gpar=gpar, nkc=nkc:
                 nc.tensor.matmul(
                     pctx[gpar][:], vfm[hp][:, kd * 128:(kd + 1) * 128],
                     pb_pf[:, s, uu * 512:(uu + 1) * 512],
                     start=(kd == 0), stop=(kd == nkc - 1),
                     skip_group_check=True),
                 inc=(sPE if kd == nkc - 1 or uu == 1 else None))
        slot_pe[s] = sPE.val
        return sPE.val

    def emit_den(h, g, pool_cnt):
        p.wait("tensor", sPO, pool_cnt)
        p.wait("tensor", sDVE, pden_free[g % 2])
        p.do("tensor", lambda g=g: nc.tensor.matmul(
            pden[g % 2][0:1, :], onecf[:, 0:1], Sa[g % 2][:],
            start=True, stop=True, skip_group_check=True), inc=sPE)
        den_pe[g % 2] = sPE.val
        return sPE.val

    def flush_chain():
        if not pending:
            return
        h, g, gpar, den_cnt, av_cnt = pending.pop(0)
        p.wait("vector", sPE, den_cnt)
        p.wait("vector", sIO, bcast_done[0])
        p.do("vector", lambda g=g: nc.vector.reciprocal(
            bca[0:1, :], pden[g % 2][0:1, :]), inc=sDVE)
        pden_free[g % 2] = sDVE.val
        p.wait("sync", sDVE, pden_free[g % 2])
        p.do("sync", lambda: nc.sync.dma_start(
            bca[1:128, :],
            bca[0:1, :].rearrange("p (o n) -> p o n", o=1).broadcast_to([1, 127, 512])),
            inc=sIO, inc_k=16)
        bcast_done[0] = sIO.val
        p.wait("vector", sIO, bcast_done[0])
        p.wait("vector", sPE, av_cnt)
        p.do("vector", lambda h=h, g=g, gpar=gpar: nc.vector.tensor_mul(
            ctx_pf[:, h, g * 512:(g + 1) * 512], pctx[gpar][:], bca[:]), inc=sDVE)
        ctx_mult_done[gpar] = sDVE.val
        st["last_ctx_mult"] = sDVE.val

    p.wait("vector", sIO, tbl_cnt)
    p.wait("tensor", sIO, tbl_cnt)
    emit_wload(0)
    emit_wload(1)
    for t in range(NTC):
        emit_qk(0, t)
    for j in range(2):
        emit_vv(0, j)
    for g in range(2):
        emit_vt(0, g)

    def run_task(task):
        kind, hb, a = task
        if kind == "qk":
            emit_qk(hb, a)
        elif kind == "vv":
            emit_vv(hb, a)
        elif kind == "vt":
            emit_vt(hb, a)
        else:
            p.wait("scalar", sPE, qkv_done[NH - 1])
            for k in range(4):
                hc = 4 * hb + k
                p.do("scalar", lambda hc=hc: nc.scalar.dma_start(
                    wo_pf[:, hc, :], wobf_ap[hc * 128:(hc + 1) * 128, :]),
                    inc=sW, inc_k=16)

    den_q = []

    def drain_den(force=False):
        while den_q and (force or st["pair"] - den_q[0][4] >= 2):
            dh, dg, pb_cnt, dav, _ = den_q.pop(0)
            den_cnt = emit_den(dh, dg, pb_cnt)
            pending.append((dh, dg, dg % 2, den_cnt, dav))

    for h in range(NH):
        if h + 2 <= NH - 1:
            emit_wload(h + 2)
        if h + 1 <= NH - 1:
            hb = h + 1
            tasks = ([("qk", hb, t) for t in range(NTC)]
                     + [("vv", hb, j) for j in range(2)]
                     + [("vt", hb, g) for g in range(2)])
        else:
            st["wo_w_base"] = sW.val
            tasks = [("wo", b, 0) for b in range(4)]
        npairs = 20
        posset = {(i * npairs) // len(tasks) for i in range(len(tasks))}

        pidx = 0
        for g in range(NTC):
            nkc = 4 * g + 4
            npr = nkc // 2
            metas = {}
            for pr in range(npr):
                metas[pr] = emit_pair(h, g, pr)
                drain_den()
                if pr == 1:
                    flush_chain()
                if pr >= 1:
                    emit_av(h, g, pr - 1, metas[pr - 1], nkc)
                if pidx in posset and tasks:
                    run_task(tasks.pop(0))
                pidx += 1
            av_cnt = emit_av(h, g, npr - 1, metas[npr - 1], nkc)
            den_q.append((h, g, sPO.val, av_cnt, st["pair"]))
        while tasks:
            run_task(tasks.pop(0))

    drain_den(force=True)
    while pending:
        flush_chain()
    ctx_done = sDVE.val
    p.emit()
    return st["wo_w_base"], ctx_done


def emit_wo(nc, sems, consts, ctxbuf, wo_res, wo_w_base, ctx_done,
            x_src_ap, x_dst_ap, ctx):
    sIO, sW, sPE, sDVE, sST = (sems[k] for k in ("io", "w", "pe", "dve", "ev"))
    xin = [ctx.enter_context(nc.sbuf_tensor(uname(f"wo_xi{i}"), [128, E], F32)) for i in range(2)]
    xout = [ctx.enter_context(nc.sbuf_tensor(uname(f"wo_xo{i}"), [128, E], F32)) for i in range(2)]
    pw = [ctx.enter_context(nc.psum_tensor(uname(f"wo_p{i}"), [128, 512], F32)) for i in range(2)]

    p = Prog(nc)
    b_io, b_pe, b_dve, b_st = sIO.val, sPE.val, sDVE.val, sST.val
    ctx_pf = pf3(ctxbuf, NH, S)
    wo_pf = pf3(wo_res, NH, E)
    wo_cnt = [0] * NT

    p.wait("sync", sIO, b_io)
    p.wait("sync", sST, b_st)
    for tt in range(NT):
        if tt >= 2:
            p.wait("sync", sDVE, b_dve + 4 * (tt - 1))
        p.do("sync", lambda tt=tt: nc.sync.dma_start(
            xin[tt % 2][:], x_src_ap[tt * 128:(tt + 1) * 128, :]), inc=sIO, inc_k=16)

        if tt == 0:
            p.wait("tensor", sDVE, ctx_done)
        for dcol in range(4):
            G = 4 * tt + dcol
            par = G % 2
            if G >= 2:
                p.wait("tensor", sDVE, b_dve + G - 1)
            for hc in range(NH):
                if tt == 0 and dcol == 0:
                    p.wait("tensor", sW, wo_w_base + 16 * (hc + 1))
                p.do("tensor", lambda tt=tt, dcol=dcol, hc=hc, par=par: nc.tensor.matmul(
                    pw[par][:], ctx_pf[:, hc, tt * 128:(tt + 1) * 128],
                    wo_pf[:, hc, dcol * 512:(dcol + 1) * 512],
                    start=(hc == 0), stop=(hc == NH - 1)),
                    inc=(sPE if hc == NH - 1 else None))
            p.wait("vector", sPE, b_pe + G + 1)
            p.wait("vector", sIO, b_io + 16 * (tt + 1))
            if tt >= 2:
                p.wait("vector", sST, b_st + 16 * (tt - 1))
            p.do("vector", lambda tt=tt, dcol=dcol, par=par: nc.vector.scalar_tensor_tensor(
                xout[tt % 2][:, dcol * 512:(dcol + 1) * 512], pw[par][:], 1.0,
                xin[tt % 2][:, dcol * 512:(dcol + 1) * 512],
                op0=ALU.mult, op1=ALU.add), inc=sDVE)
        p.wait("gpsimd", sDVE, b_dve + 4 * (tt + 1))
        p.do("gpsimd", lambda tt=tt: nc.gpsimd.dma_start(
            x_dst_ap[tt * 128:(tt + 1) * 128, :], xout[tt % 2][:]), inc=sST, inc_k=16)
        wo_cnt[tt] = sST.val
    p.emit()
    return wo_cnt


def build(debug=False):
    nc = bass.Bass("TRN2", target_bir_lowering=False)

    x_in = nc.dram_tensor("x", [S, E], F32, kind="ExternalInput").ap()
    wqkv = nc.dram_tensor("wqkv", [NH, 3, 128, E], F32, kind="ExternalInput").ap()
    wo = nc.dram_tensor("wo", [E, E], F32, kind="ExternalInput").ap()
    wgup = nc.dram_tensor("wgup", [NI, 2, 128, E], F32, kind="ExternalInput").ap()
    wdn = nc.dram_tensor("wdn", [NHG, NI, 128, 512], F32, kind="ExternalInput").ap()
    c2_d = nc.dram_tensor("c2", [128, S], BF16, kind="ExternalInput").ap()
    spm_d = nc.dram_tensor("spm", [128, S], BF16, kind="ExternalInput").ap()
    mask_d = nc.dram_tensor("mask", [128, 2048], BF16, kind="ExternalInput").ap()
    idbf_d = nc.dram_tensor("idbf", [128, 128], BF16, kind="ExternalInput").ap()
    onecf_d = nc.dram_tensor("onecf", [128, 1], F32, kind="ExternalInput").ap()

    y = nc.dram_tensor("y", [S, E], F32, kind="ExternalOutput").ap()
    x1 = nc.dram_tensor("x1", [S, E], F32, kind="Internal").ap()
    x2 = nc.dram_tensor("x2", [S, E], F32, kind="Internal").ap()
    h_buf = nc.dram_tensor("h_buf", [NI, 128, S], BF16, kind="Internal").ap()
    wdbf = nc.dram_tensor("wdbf", [NHG, NI, 128, 512], BF16, kind="Internal").ap()
    wobf = nc.dram_tensor("wobf", [E, E], BF16, kind="Internal").ap()

    with ExitStack() as top:
        sems = {k: Sem(nc, top, f"s_{k}")
                for k in ("io", "w", "w2", "pe", "act", "dve", "ev", "fv", "fa", "po")}

        idbf = top.enter_context(nc.sbuf_tensor("sb_idbf", [128, 128], BF16))
        onecf = top.enter_context(nc.sbuf_tensor("sb_onecf", [128, 1], F32))
        epsc = top.enter_context(nc.sbuf_tensor("sb_epsc", [128, 1], F32))
        cbias = top.enter_context(nc.sbuf_tensor("sb_cbias", [128, 1], F32))
        consts = dict(idbf=idbf, onecf=onecf, epsc=epsc, cbias=cbias)

        p = Prog(nc)
        for dst, src in ((idbf, idbf_d), (onecf, onecf_d)):
            p.do("sync", lambda dst=dst, src=src: nc.sync.dma_start(dst[:], src),
                 inc=sems["io"], inc_k=16)
        p.do("gpsimd", lambda: nc.gpsimd.memset(epsc[:], EPS), inc=sems["dve"])
        p.do("gpsimd", lambda: nc.gpsimd.memset(cbias[:], CBIAS_V), inc=sems["dve"])
        p.wait("sync", sems["io"], sems["io"].val)
        p.wait("sync", sems["dve"], sems["dve"].val)
        p.emit()

        with ExitStack() as sA:
            xnt = sA.enter_context(nc.sbuf_tensor("xnt1", [128, NE * S], BF16))
            with ExitStack() as c1:
                ln1 = emit_ln(nc, sems, consts, xnt, x_in, c1)
            with ExitStack() as c2s:
                emit_gateup(nc, sems, consts, xnt, wgup, h_buf, c2s,
                            precast=(wdn, wdbf, wo, wobf), ln_cnt=ln1)
        with ExitStack() as c3:
            dn1_cnt = emit_down(nc, sems, consts, wdbf, h_buf, x_in, x1, c3)

        with ExitStack() as sB:
            ctxbuf = sB.enter_context(nc.sbuf_tensor("ctxbuf", [128, NH * S], BF16))
            c2 = sB.enter_context(nc.sbuf_tensor("sb_c2", [128, S], BF16))
            spm = sB.enter_context(nc.sbuf_tensor("sb_spm", [128, S], BF16))
            mask = sB.enter_context(nc.sbuf_tensor("sb_mask", [128, 2048], BF16))
            consts2 = dict(consts, c2=c2, spm=spm, mask=mask)
            pt = Prog(nc)
            pt.wait("sync", sems["ev"], sems["ev"].val)
            pt.wait("sync", sems["io"], sems["io"].val)
            for dst, srcd in ((c2, c2_d), (spm, spm_d), (mask, mask_d)):
                pt.do("sync", lambda dst=dst, srcd=srcd: nc.sync.dma_start(
                    dst[:], srcd), inc=sems["io"], inc_k=16)
            pt.emit()
            tbl_cnt = sems["io"].val
            with ExitStack() as sB2:
                xnt = sB2.enter_context(nc.sbuf_tensor("xnt2", [128, NE * S], BF16))
                with ExitStack() as c4:
                    ln2 = emit_ln(nc, sems, consts2, xnt, x1, c4)
                with ExitStack() as c5:
                    wo_w_base, ctx_done = emit_attn(
                        nc, sems, consts2, xnt, wqkv, ctxbuf, wobf, ln2,
                        tbl_cnt, c5)
                with ExitStack() as c6:
                    wo_cnt = emit_wo(nc, sems, consts2, ctxbuf, xnt, wo_w_base,
                                     ctx_done, x1, x2, c6)

        with ExitStack() as sC:
            xnt = sC.enter_context(nc.sbuf_tensor("xnt3", [128, NE * S], BF16))
            with ExitStack() as c7:
                ln3 = emit_ln(nc, sems, consts, xnt, x2, c7)
            with ExitStack() as c8:
                emit_gateup(nc, sems, consts, xnt, wgup, h_buf, c8, ln_cnt=ln3)
        with ExitStack() as c9:
            emit_down(nc, sems, consts, wdbf, h_buf, x2, y, c9)

        p = Prog(nc)
        p.wait("sync", sems["io"], sems["io"].val)
        p.wait("sync", sems["w"], sems["w"].val)
        p.wait("sync", sems["w2"], sems["w2"].val)
        p.wait("sync", sems["ev"], sems["ev"].val)
        p.emit()

    return nc


def host_inputs(L0_attn_Wqkv, L0_attn_Wo, L0_ffn_w_gate_up, L0_ffn_w_down):
    f32 = np.float32
    bf16 = ml_dtypes.bfloat16

    Wqkv = np.asarray(L0_attn_Wqkv, dtype=f32)
    perm = np.concatenate([np.arange(0, 128, 2), np.arange(1, 128, 2)])
    wqkv_r = np.empty((NH, 3, 128, E), dtype=f32)
    for h in range(NH):
        for wi, base in enumerate((0, E, 2 * E)):
            m = Wqkv[:, base + h * 128: base + (h + 1) * 128]
            if wi < 2:
                m = m[:, perm]
            wqkv_r[h, wi] = m.reshape(NE, 128, 128).transpose(1, 0, 2).reshape(128, E)

    wo_r = np.ascontiguousarray(np.asarray(L0_attn_Wo, dtype=f32))

    T = np.ascontiguousarray(np.asarray(L0_ffn_w_gate_up, dtype=f32).T)
    wgup_r = np.empty((NI, 2, 128, E), dtype=f32)
    for i in range(NI):
        g = T[:, i * 128:(i + 1) * 128]
        u = T[:, INTER + i * 128: INTER + (i + 1) * 128]
        wgup_r[i, 0] = g.reshape(NE, 128, 128).transpose(1, 0, 2).reshape(128, E)
        wgup_r[i, 1] = u.reshape(NE, 128, 128).transpose(1, 0, 2).reshape(128, E)

    WT = np.ascontiguousarray(np.asarray(L0_ffn_w_down, dtype=f32).T)
    wdn_r = np.ascontiguousarray(
        WT.reshape(NI, 128, NHG, 512).transpose(2, 0, 1, 3))

    i_idx = np.arange(64, dtype=f32)
    inv_freq = (1.0 / (10000.0 ** (2.0 * i_idx / 128.0))).astype(f32)
    pos = np.arange(S, dtype=f32)
    th = np.outer(pos, inv_freq).astype(f32)
    cos = np.cos(th).astype(f32).T
    sin = np.sin(th).astype(f32).T
    c2 = np.concatenate([cos, cos], axis=0)
    spm = np.concatenate([-sin, sin], axis=0).astype(f32)

    mask = np.zeros((128, 4, 512), dtype=f32)
    kr = np.arange(128)[:, None]
    qc = np.arange(512)[None, :]
    for j in range(4):
        mask[:, j, :] = np.where(128 * j + kr <= qc, 0.0, -452.0)
    mask = np.ascontiguousarray(mask.reshape(128, 2048)).astype(bf16)

    return {
        "wqkv": wqkv_r, "wo": wo_r, "wgup": wgup_r, "wdn": wdn_r,
        "c2": np.ascontiguousarray(c2).astype(bf16),
        "spm": np.ascontiguousarray(spm).astype(bf16),
        "mask": mask,
        "idbf": np.eye(128, dtype=bf16),
        "onecf": np.ones((128, 1), dtype=np.float32),
    }


def kernel(L0_attn_x, L0_attn_Wqkv, L0_attn_Wo, L0_ffn_w_gate_up, L0_ffn_w_down):
    xs = np.ascontiguousarray(np.asarray(L0_attn_x, dtype=np.float32))
    shared = host_inputs(L0_attn_Wqkv, L0_attn_Wo, L0_ffn_w_gate_up, L0_ffn_w_down)
    nc = build(debug=False)
    in_maps = [dict(shared, x=np.ascontiguousarray(xs[b])) for b in range(8)]
    res = run_bass_kernel_spmd(nc, in_maps, core_ids=list(range(8)))
    return np.stack([r["y"] for r in res.results], axis=0)
